# version 11
# baseline (speedup 1.0000x reference)
import numpy as np
import ml_dtypes
from contextlib import ExitStack

import concourse.bass as bass
import concourse.mybir as mybir
from concourse.bass_utils import run_bass_kernel_spmd

F32 = mybir.dt.float32
BF16 = mybir.dt.bfloat16
AF = mybir.ActivationFunctionType
ALU = mybir.AluOpType
AX = mybir.AxisListType

T = 4096
D = 1024
DC = D // 128
EUV = 2048
H = 4096
EPS = 1e-5
MW = 8064


class Prog:

    ENGS = ("sync", "scalar", "vector", "tensor", "gpsimd")

    def __init__(self, nc):
        self.nc = nc
        self.ops = {e: [] for e in self.ENGS}
        self.cnt = {}
        self.hw = {}
        self.sems = {}

    def _sem(self, name):
        if name not in self.sems:
            self.sems[name] = self.nc.alloc_semaphore(name)
            self.cnt[name] = 0
        return self.sems[name]

    def emit(self, eng, fn, waits=(), inc=None):
        best = {}
        for wt in waits:
            if wt is None:
                continue
            s, v = wt[0], wt[1]
            peng = wt[2] if len(wt) > 2 else None
            if v <= 0 or peng == eng:
                continue
            if best.get(s, -1) < v:
                best[s] = v
        wl = []
        for s, v in best.items():
            k = (eng, s)
            if self.hw.get(k, 0) >= v:
                continue
            self.hw[k] = v
            wl.append((s, v))
        tok = None
        if inc is not None:
            s, amt = inc
            self._sem(s)
            self.cnt[s] += amt
            tok = (s, self.cnt[s], eng if amt == 1 else None)
        self.ops[eng].append((wl, fn, inc))
        return tok

    def dma(self, eng, out, in_, sem_name, waits=()):
        self._sem(sem_name)
        return self.emit(eng, (lambda e, o=out, i=in_: e.dma_start(out=o, in_=i)),
                         waits=waits, inc=(sem_name, 16))

    def wait(self, eng, waits):
        self.emit(eng, None, waits=waits)

    def cur_all(self):
        return [(s, c, None) for s, c in self.cnt.items() if c > 0]

    def join_all(self):
        toks = self.cur_all()
        for e in self.ENGS:
            self.wait(e, toks)

    def build(self):
        with self.nc.Block() as block:
            for eng in self.ENGS:
                ops = self.ops[eng]
                if not ops:
                    continue
                getattr(block, eng)(lambda e, ops=ops: self._replay(e, ops))

    def _replay(self, e, ops):
        for wl, fn, inc in ops:
            for (s, v) in wl:
                e.wait_ge(self.sems[s], v)
            if fn is not None:
                inst = fn(e)
                if inc is not None:
                    inst.then_inc(self.sems[inc[0]], inc[1])

    def n_ops(self):
        return sum(len(v) for v in self.ops.values())


def build_nc(phases=("p0", "p1", "p3", "ffn"), dbg=(), serial=False):
    nc = bass.Bass("TRN2", target_bir_lowering=False)

    def din(name, shape, dt=F32):
        return nc.dram_tensor(name, shape, dt, kind="ExternalInput").ap()

    x_h = din("x", [T, D])
    uvwT_h = din("uvwT", [D, 4224])
    owT_h = din("owT", [EUV, D])
    wr_h = din("wr", [D, 8])
    wup_h = din("wup", [8, D, H])
    wdn_h = din("wdn", [8, H, D])
    wsu_h = din("wsu", [D, H])
    wsd_h = din("wsd", [H, D])
    cols_h = din("cols", [128, 64])
    lnw_h = din("lnw", [128, D])
    lnb_h = din("lnb", [128, D])
    mbias_h = din("mbias", [128, MW])
    qcos_h = din("qcos", [128, T])
    qsin_h = din("qsin", [128, T])
    kcos_h = din("kcos", [128, T])
    ksin_h = din("ksin", [128, T])
    uvbrow_h = din("uvbrow_f", [1, 4224])
    ones1_h = din("ones1", [1, 128], BF16)
    perm_h = din("perm", [128, 128], BF16)
    identf_h = din("identf", [128, 128])

    out_h = nc.dram_tensor("out", [T, D], F32, kind="ExternalOutput").ap()

    def scratch(name, shape, dt):
        kind = "ExternalOutput" if name in dbg else "Internal"
        return nc.dram_tensor(name, shape, dt, kind=kind).ap()

    uvwTb = nc.dram_tensor("uvwTb", [D, 4224], BF16, kind="Internal").ap()
    owTb = nc.dram_tensor("owTb", [EUV, D], BF16, kind="Internal").ap()
    wupb = nc.dram_tensor("wupb", [8, D, H], BF16, kind="Internal").ap()
    wdnb = nc.dram_tensor("wdnb", [8, H, D], BF16, kind="Internal").ap()
    wsub = nc.dram_tensor("wsub", [D, H], BF16, kind="Internal").ap()
    wsdb = nc.dram_tensor("wsdb", [H, D], BF16, kind="Internal").ap()

    xn_hbm = scratch("xn_s", [DC, 128, T], BF16)
    xntm_hbm = scratch("xntm_s", [T, D], F32) if "xntm_s" in dbg else None
    r0_hbm = scratch("r0_s", [T, D], F32)
    at2_hbm = scratch("at2_s", [T, D], BF16)
    u_hbm = scratch("u_s", [16, 128, T], BF16)
    v_hbm = scratch("v_s", [32, 128, EUV], BF16)

    P = Prog(nc)
    RI = float(1.0 / D)

    with ExitStack() as top:
        def sb(name, shape, dt=F32, ctx=None):
            return (ctx or top).enter_context(nc.sbuf_tensor(name, shape, dt))

        cols = sb("cols_b", [128, 64])
        identf = sb("identf_b", [128, 128])
        perm = sb("perm_b", [128, 128], BF16)
        wr_sb = sb("wr_sb", [128, 64])
        logits = sb("logits", [128, 32 * 8])
        comb = sb("comb", [128, 32 * 8])
        q_fm = sb("q_fm", [128, T], BF16)
        k_fm = sb("k_fm", [128, T], BF16)
        PS = [top.enter_context(nc.psum_tensor(f"ps{i}", [128, 512], F32))
              for i in range(8)]
        ps_wr = [None] * 8
        ps_rd = [None] * 8

        P.dma("sync", cols[:], cols_h, "s_const")
        P.dma("sync", identf[:], identf_h, "s_const")
        P.dma("sync", perm[:], perm_h, "s_const")
        P.dma("sync", wr_sb[:].rearrange("p (c e) -> p c e", e=8),
              wr_h.rearrange("(c p) e -> p c e", p=128), "s_const")

        uvbrow_hi_tok = None
        if "p1" in phases:
            uvbrow_sb_early = top.enter_context(
                nc.sbuf_tensor("uvbrow_b", [1, 4224], BF16))
            uvbrow_hi_tok = P.dma("gpsimd", uvbrow_sb_early[:], uvbrow_h,
                                  "s_bhi")
        uvwT_cast = P.dma("gpsimd", uvwTb, uvwT_h, "swc")
        owT_cast = P.dma("gpsimd", owTb, owT_h, "swc", waits=[uvwT_cast])
        sh_cast = P.dma("gpsimd", wsub, wsu_h, "swc", waits=[owT_cast])
        sh_cast = P.dma("gpsimd", wsdb, wsd_h, "swc", waits=[sh_cast])
        exp_cast = []
        prev = sh_cast
        for e_ in range(8):
            prev = P.dma("gpsimd", wupb[e_], wup_h[e_], "swc", waits=[prev])
            prev = P.dma("gpsimd", wdnb[e_], wdn_h[e_], "swc", waits=[prev])
            exp_cast.append(prev)
        wcast_tok = ("swc", P.cnt["swc"], None)

        def mmf(psap, lh, rh, st, sp, skip=False):
            return (lambda e, a=psap, b=lh, c=rh, s=st, p=sp, k=skip:
                    e.matmul(a, b, c, start=s, stop=p, skip_group_check=k))

        def vcopy(o, i):
            return (lambda e, a=o, b=i: e.tensor_copy(out=a, in_=b))

        def scopy(o, i):
            return (lambda e, a=o, b=i: e.copy(out=a, in_=b))

        cA = top.enter_context(ExitStack())
        xn2_fm = sb("xn2_fm", [128, DC * T], BF16, ctx=cA)
        do_p1 = "p1" in phases
        if do_p1:
            slab = [sb(f"slab{i}", [128, 8 * 512], BF16, ctx=cA) for i in range(2)]
            ubuf = [sb(f"ubuf{i}", [128, 2048], BF16, ctx=cA) for i in range(2)]
            vasm = [sb(f"vasm{i}", [128, 2048], BF16, ctx=cA) for i in range(2)]
            ropeb = [sb("ropeb0", [128, 4 * 512], ctx=cA)]
            ropeb.append(ropeb[0])
            uvbrow = uvbrow_sb_early
            uvbrow_lo = sb("uvbrow_lo", [1, 4224], BF16, ctx=cA)
            ones1 = sb("ones1_b", [1, 128], BF16, ctx=cA)
            bsf = [sb(f"bsf{i}", [128, 512], ctx=cA) for i in range(2)]
            qpf = sb("qpf", [128, 512], ctx=cA)
            kpf = sb("kpf", [128, 512], ctx=cA)
            qpb = sb("qpb", [128, 512], BF16, ctx=cA)
            kpb = sb("kpb", [128, 512], BF16, ctx=cA)
            tm1 = sb("tm1", [128, 512], ctx=cA)
            tm2 = sb("tm2", [128, 512], ctx=cA)

        if "p0" in phases:
            c0 = top.enter_context(ExitStack())
            lnw_bc = sb("lnw_bc", [128, D], ctx=c0)
            lnb_bc = sb("lnb_bc", [128, D], ctx=c0)
            xt = [sb(f"xt{i}", [128, D], ctx=c0) for i in range(2)]
            xnf = [sb(f"xnf{i}", [128, D], ctx=c0) for i in range(2)]
            sqb = sb("sqb", [128, D], ctx=c0)
            t2 = sb("t2", [128, D], ctx=c0)
            x2f = sb("x2f", [128, D], ctx=c0)
            xfm32 = sb("xfm32", [128, D], ctx=c0)
            stg = [sb(f"stg{i}", [128, D], BF16, ctx=c0) for i in range(2)]
            r0b = [sb(f"r0b{i}", [128, D], ctx=c0) for i in range(2)]
            scl = sb("scl", [128, 16], ctx=c0)

            P.dma("sync", lnw_bc[:], lnw_h, "s_const")
            P.dma("sync", lnb_bc[:], lnb_h, "s_const")
            const_tok = ("s_const", P.cnt["s_const"], None)
            eps_col = cols[:, 45:46]
            ri_col = cols[:, 48:49]
            nh_col = cols[:, 47:48]

            sx_tok = [None, None]
            sr0_tok = [None, None]
            sxn_tok = [None, None]
            xt_rd = [None, None]
            xnf_rd = [None, None]
            x2f_rd = None
            xfm_rd = None
            xn2_tok = [None] * 32

            def emit_p0_chunk(tc):
                nonlocal x2f_rd, xfm_rd
                p = tc % 2
                t0 = tc * 128
                sx_tok[p] = P.dma("sync", xt[p][:], x_h[t0:t0 + 128, :],
                                  f"sx{p}", waits=[xt_rd[p], sr0_tok[p]])
                w0 = [sx_tok[p], const_tok]
                ts1 = P.emit("vector", lambda e, a=scl[:, 0:1], b=xt[p][:]:
                             e.reduce_sum(a, b, AX.X), waits=w0,
                             inc=("dve", 1))
                tmu = P.emit("scalar",
                             lambda e, o=scl[:, 1:2], i=scl[:, 0:1], s=ri_col:
                             e.activation(o, i, AF.Identity, scale=s),
                             waits=[ts1], inc=("act", 1))
                P.emit("vector", lambda e, a=xt[p][:], s=scl[:, 1:2]:
                       e.tensor_scalar(a, a, s, None, ALU.subtract),
                       waits=[tmu])
                P.emit("vector", lambda e, o=sqb[:], i=xt[p][:]:
                       e.tensor_mul(o, i, i))
                ts2 = P.emit("vector", lambda e, o=scl[:, 2:3], i=sqb[:]:
                             e.reduce_sum(o, i, AX.X), inc=("dve", 1))
                P.emit("scalar",
                       lambda e, o=scl[:, 3:4], i=scl[:, 2:3], s=ri_col,
                       b=eps_col: e.activation(o, i, AF.Ln, scale=s, bias=b),
                       waits=[ts2])
                trs = P.emit("scalar",
                             lambda e, o=scl[:, 4:5], i=scl[:, 3:4], s=nh_col:
                             e.activation(o, i, AF.Exp, scale=s),
                             inc=("act", 1))
                P.emit("vector", lambda e, o=xnf[p][:], i=xt[p][:], s=scl[:, 4:5]:
                       e.tensor_scalar_mul(o, i, s), waits=[xnf_rd[p], trs])
                xt_rd[p] = P.emit(
                    "vector",
                    lambda e, o=r0b[p][:], i=xt[p][:], s=scl[:, 1:2], j=xnf[p][:]:
                    e.scalar_tensor_tensor(o, i, s, j, ALU.add, ALU.add),
                    waits=[sr0_tok[p]], inc=("dve", 1))
                sr0_tok[p] = P.dma("sync", r0_hbm[t0:t0 + 128, :], r0b[p][:],
                                   f"sr0{p}", waits=[xt_rd[p]])
                ts3 = P.emit("vector", lambda e, a=scl[:, 6:7], b=xnf[p][:]:
                             e.reduce_sum(a, b, AX.X), inc=("dve", 1))
                tmu2 = P.emit("scalar",
                              lambda e, o=scl[:, 7:8], i=scl[:, 6:7], s=ri_col:
                              e.activation(o, i, AF.Identity, scale=s),
                              waits=[ts3], inc=("act", 1))
                P.emit("vector", lambda e, o=t2[:], i=xnf[p][:], s=scl[:, 7:8]:
                       e.tensor_scalar(o, i, s, None, ALU.subtract),
                       waits=[x2f_rd, tmu2])
                P.emit("vector", lambda e, o=sqb[:], i=t2[:]:
                       e.tensor_mul(o, i, i))
                ts4 = P.emit("vector", lambda e, o=scl[:, 8:9], i=sqb[:]:
                             e.reduce_sum(o, i, AX.X), inc=("dve", 1))
                P.emit("scalar",
                       lambda e, o=scl[:, 9:10], i=scl[:, 8:9], s=ri_col,
                       b=eps_col: e.activation(o, i, AF.Ln, scale=s, bias=b),
                       waits=[ts4])
                trs2 = P.emit("scalar",
                              lambda e, o=scl[:, 10:11], i=scl[:, 9:10], s=nh_col:
                              e.activation(o, i, AF.Exp, scale=s),
                              inc=("act", 1))
                P.emit("vector", lambda e, a=t2[:], s=scl[:, 10:11]:
                       e.tensor_scalar_mul(a, a, s), waits=[trs2])
                P.emit("vector", lambda e, a=t2[:], b=lnw_bc[:]:
                       e.tensor_mul(a, a, b))
                tx2f = P.emit("vector", lambda e, o=x2f[:], a=t2[:], b=lnb_bc[:]:
                              e.tensor_add(o, a, b), inc=("dve", 1))

                ev_last = None
                for c8 in range(8):
                    bk = c8 % 2
                    ttr = P.emit("tensor",
                                 lambda e, o=PS[bk][:, 0:128],
                                 i=xnf[p][:, c8 * 128:(c8 + 1) * 128], d=identf[:]:
                                 e.transpose(o, i, d),
                                 waits=[ps_rd[bk], const_tok, xt_rd[p]],
                                 inc=("pe", 1))
                    ps_wr[bk] = ttr
                    P.emit("scalar", scopy(stg[p][:, c8 * 128:(c8 + 1) * 128],
                                           PS[bk][:, 0:128]),
                           waits=[ttr, sxn_tok[p]])
                    ev_last = P.emit("scalar",
                                     scopy(xfm32[:, c8 * 128:(c8 + 1) * 128],
                                           PS[bk][:, 0:128]),
                                     waits=[xfm_rd], inc=("act", 1))
                    ps_rd[bk] = ev_last
                xnf_rd[p] = ev_last
                sxn_tok[p] = P.dma(
                    "sync",
                    xn_hbm[:, :, t0:t0 + 128].rearrange("c p t -> p c t"),
                    stg[p][:].rearrange("p (c t) -> p c t", t=128),
                    f"sxn{p}", waits=[ev_last])
                rmm = None
                for c8 in range(8):
                    rmm = P.emit("tensor",
                                 mmf(PS[4][:, 0:8], xfm32[:, c8 * 128:(c8 + 1) * 128],
                                     wr_sb[:, c8 * 8:(c8 + 1) * 8],
                                     c8 == 0, c8 == 7),
                                 waits=[ev_last, ps_rd[4], const_tok],
                                 inc=("pe", 1))
                ps_wr[4] = rmm
                xfm_rd = rmm
                lgt = P.emit("vector", vcopy(logits[:, tc * 8:(tc + 1) * 8],
                                             PS[4][:, 0:8]),
                             waits=[rmm], inc=("dve", 1))
                ps_rd[4] = lgt
                ev2 = None
                for c8 in range(8):
                    bk = 2 + c8 % 2
                    ttr = P.emit("tensor",
                                 lambda e, o=PS[bk][:, 0:128],
                                 i=x2f[:, c8 * 128:(c8 + 1) * 128], d=identf[:]:
                                 e.transpose(o, i, d),
                                 waits=[ps_rd[bk], tx2f, const_tok],
                                 inc=("pe", 1))
                    ps_wr[bk] = ttr
                    ev2 = P.emit("vector",
                                 vcopy(xn2_fm[:, c8 * T + t0:c8 * T + t0 + 128],
                                       PS[bk][:, 0:128]),
                                 waits=[ttr], inc=("dve", 1))
                    ps_rd[bk] = ev2
                    x2f_rd = ttr
                xn2_tok[tc] = ev2

            slab_tok = [None, None]
            slab_rd = [None, None]
            ub_tok = [None, None]
            va_tok = [None, None]
            rp_tok = [None, None]
            rp_rd = [None, None]
            uvbrow_tok = None
            p1_state = {"prev_q": None, "qpb_rd": None, "kpb_rd": None,
                        "bsf_rd": [None, None]}

            def emit_p1_tile(tt):
                t0 = tt * 512
                xtok = xn2_tok[tt * 4 + 3]
                rsl = 0
                rp_tok[rsl] = P.dma(
                    "sync", ropeb[rsl][:, 0:512], qcos_h[:, t0:t0 + 512],
                    f"srp{rsl}", waits=[rp_rd[rsl]])
                P.dma("sync", ropeb[rsl][:, 512:1024], qsin_h[:, t0:t0 + 512],
                      f"srp{rsl}")
                P.dma("sync", ropeb[rsl][:, 1024:1536], kcos_h[:, t0:t0 + 512],
                      f"srp{rsl}")
                rp_tok[rsl] = P.dma(
                    "sync", ropeb[rsl][:, 1536:2048], ksin_h[:, t0:t0 + 512],
                    f"srp{rsl}")
                for os_ in range(9):
                    sl = os_ % 2
                    ow = 128 if os_ == 8 else 512
                    slab_tok[sl] = P.dma(
                        "scalar",
                        slab[sl][:, 0:8 * ow].rearrange("p (c j) -> p c j", j=ow),
                        uvwTb[:, os_ * 512:os_ * 512 + ow].rearrange(
                            "(c p) j -> p c j", p=128),
                        f"suv{sl}", waits=[slab_rd[sl], uvwT_cast])
                    if os_ < 4:
                        evl = None
                        for ob in range(4):
                            bk = ob % 2
                            mmt = None
                            for dc in range(DC):
                                mmt = P.emit(
                                    "tensor",
                                    mmf(PS[bk][:, 0:512],
                                        slab[sl][:, dc * 512 + ob * 128:
                                                 dc * 512 + (ob + 1) * 128],
                                        xn2_fm[:, dc * T + t0:dc * T + t0 + 512],
                                        dc == 0, dc == 7),
                                    waits=[slab_tok[sl], ps_rd[bk], xtok],
                                    inc=("pe", 1) if dc == 7 else None)
                            ps_wr[bk] = mmt
                            slab_rd[sl] = mmt
                            och = os_ * 4 + ob
                            evl = P.emit(
                                "scalar",
                                lambda e, o=ubuf[sl][:, ob * 512:(ob + 1) * 512],
                                i=PS[bk][:, 0:512], b=cols[:, och:och + 1]:
                                e.activation(o, i, AF.Silu, bias=b),
                                waits=[mmt, ub_tok[sl]], inc=("act", 1))
                            ps_rd[bk] = evl
                        ub_tok[sl] = P.dma(
                            "sync",
                            u_hbm[os_ * 4:(os_ + 1) * 4, :, t0:t0 + 512].rearrange(
                                "c p t -> p c t"),
                            ubuf[sl][:].rearrange("p (c t) -> p c t", t=512),
                            f"suo{sl}", waits=[evl])
                    elif os_ < 8:
                        ev2v = None
                        for tc4 in range(4):
                            bk = tc4 % 2
                            mmt = None
                            for dc in range(DC):
                                mmt = P.emit(
                                    "tensor",
                                    mmf(PS[bk][:, 0:512],
                                        xn2_fm[:, dc * T + t0 + tc4 * 128:
                                               dc * T + t0 + (tc4 + 1) * 128],
                                        slab[sl][:, dc * 512:(dc + 1) * 512],
                                        dc == 0, False),
                                    waits=[slab_tok[sl], ps_rd[bk], xtok],
                                    inc=None)
                            P.emit(
                                "tensor",
                                mmf(PS[bk][:, 0:512],
                                    ones1[:, 0:128],
                                    uvbrow[:, os_ * 512:(os_ + 1) * 512],
                                    False, False),
                                waits=[uvbrow_tok])
                            mmt = P.emit(
                                "tensor",
                                mmf(PS[bk][:, 0:512],
                                    ones1[:, 0:128],
                                    uvbrow_lo[:, os_ * 512:(os_ + 1) * 512],
                                    False, True),
                                inc=("pe", 1))
                            ps_wr[bk] = mmt
                            slab_rd[sl] = mmt
                            ev2v = P.emit(
                                "scalar",
                                lambda e, o=vasm[sl][:, tc4 * 512:(tc4 + 1) * 512],
                                i=PS[bk][:, 0:512]: e.activation(o, i, AF.Silu),
                                waits=[mmt, va_tok[sl]], inc=("act", 1))
                            ps_rd[bk] = ev2v
                        va_tok[sl] = P.dma(
                            "sync",
                            v_hbm[tt * 4:(tt + 1) * 4, :,
                                  (os_ - 4) * 512:(os_ - 3) * 512].rearrange(
                                      "c p t -> p c t"),
                            vasm[sl][:].rearrange("p (c t) -> p c t", t=512),
                            f"svo{sl}", waits=[ev2v])
                    else:
                        st_ = p1_state
                        bk = tt % 2
                        mmt = None
                        for dc in range(DC):
                            mmt = P.emit(
                                "tensor",
                                mmf(PS[bk][:, 0:512],
                                    slab[sl][:, dc * 128:(dc + 1) * 128],
                                    xn2_fm[:, dc * T + t0:dc * T + t0 + 512],
                                    dc == 0, dc == 7),
                                waits=[slab_tok[sl], ps_rd[bk], xtok],
                                inc=("pe", 1) if dc == 7 else None)
                        ps_wr[bk] = mmt
                        slab_rd[sl] = mmt
                        tb = P.emit("scalar",
                                    lambda e, o=bsf[bk][:], i=PS[bk][:, 0:512],
                                    b=cols[:, 32:33]:
                                    e.activation(o, i, AF.Silu, bias=b),
                                    waits=[mmt, st_["bsf_rd"][bk]],
                                    inc=("act", 1))
                        ps_rd[bk] = tb
                        tq = P.emit("vector",
                                    lambda e, o=qpf[:], i=bsf[bk][:],
                                    s1=cols[:, 41:42], s2=cols[:, 43:44]:
                                    e.tensor_scalar(o, i, s1, s2, ALU.mult, ALU.add),
                                    waits=[tb], inc=("dve", 1))
                        tk = P.emit("vector",
                                    lambda e, o=kpf[:], i=bsf[bk][:],
                                    s1=cols[:, 42:43], s2=cols[:, 44:45]:
                                    e.tensor_scalar(o, i, s1, s2, ALU.mult, ALU.add),
                                    inc=("dve", 1))
                        st_["bsf_rd"][bk] = tk
                        tqb = P.emit("scalar", scopy(qpb[:], qpf[:]),
                                     waits=[tq, st_["qpb_rd"]], inc=("act", 1))
                        tkb = P.emit("scalar", scopy(kpb[:], kpf[:]),
                                     waits=[tk, st_["kpb_rd"]], inc=("act", 1))
                        swq = P.emit("tensor", mmf(PS[5][:, 0:512], perm[:], qpb[:],
                                                   True, True),
                                     waits=[tqb, ps_rd[5]], inc=("pe", 1))
                        swk = P.emit("tensor", mmf(PS[6][:, 0:512], perm[:], kpb[:],
                                                   True, True),
                                     waits=[tkb, ps_rd[6]], inc=("pe", 1))
                        st_["qpb_rd"] = swq
                        st_["kpb_rd"] = swk
                        P.emit("vector", lambda e, o=tm1[:], a=qpf[:],
                               b=ropeb[rsl][:, 0:512]: e.tensor_mul(o, a, b),
                               waits=[rp_tok[rsl]])
                        P.emit("vector", lambda e, o=tm2[:], a=PS[5][:, 0:512],
                               b=ropeb[rsl][:, 512:1024]: e.tensor_mul(o, a, b),
                               waits=[swq])
                        tqf = P.emit("vector", lambda e, o=q_fm[:, t0:t0 + 512],
                                     a=tm1[:], b=tm2[:]: e.tensor_add(o, a, b),
                                     inc=("dve", 1))
                        ps_rd[5] = tqf
                        P.emit("vector", lambda e, o=tm1[:], a=kpf[:],
                               b=ropeb[rsl][:, 1024:1536]: e.tensor_mul(o, a, b))
                        P.emit("vector", lambda e, o=tm2[:], a=PS[6][:, 0:512],
                               b=ropeb[rsl][:, 1536:2048]: e.tensor_mul(o, a, b),
                               waits=[swk])
                        tkf = P.emit("vector", lambda e, o=k_fm[:, t0:t0 + 512],
                                     a=tm1[:], b=tm2[:]: e.tensor_add(o, a, b),
                                     inc=("dve", 1))
                        ps_rd[6] = tkf
                        rp_rd[rsl] = tkf
                        st_["prev_q"] = tkf

            if do_p1:
                hi_t = uvbrow_hi_tok
                P.dma("sync", ones1[:], ones1_h, "s_bias")
                lo_t = None
                for ch in range(9):
                    cw = 128 if ch == 8 else 512
                    ft = P.dma("sync", tm1[0:1, 0:cw],
                               uvbrow_h[:, ch * 512:ch * 512 + cw], "s_bias",
                               waits=[lo_t])
                    lo_t = P.emit("vector",
                                  lambda e, o=uvbrow_lo[:, ch * 512:ch * 512 + cw],
                                  a=tm1[0:1, 0:cw],
                                  b=uvbrow[:, ch * 512:ch * 512 + cw]:
                                  e.tensor_sub(o, a, b),
                                  waits=[hi_t, ft], inc=("dve", 1))
                uvbrow_tok = lo_t

            for tt in range(8):
                for tc in range(tt * 4, tt * 4 + 4):
                    emit_p0_chunk(tc)
                if do_p1:
                    emit_p1_tile(tt)

            tkv = xfm32[:, 0:128]
            mb1 = sqb[:, 0:256]
            mb2 = sqb[:, 256:512]
            mb3 = t2[:, 0:256]
            e1b = t2[:, 256:512]
            l2b = x2f[:, 0:256]
            pbb = x2f[:, 256:512]

            def as3(ap):
                return ap.rearrange("p (c e) -> p c e", e=8)

            def bc3(ap):
                return ap.rearrange("p (c e) -> p c e", e=1).broadcast_to(
                    (128, 32, 8))

            lgv = logits[:]
            t1 = P.emit("vector", lambda e, o=tkv[:, 0:32], i=lgv:
                        e.reduce_max(o.rearrange("p (c e) -> p c e", e=1),
                                     as3(i), AX.X), inc=("dve", 1))
            t2_ = P.emit("scalar", lambda e, o=mb1[:], i=tkv[:, 0:32]:
                         e.activation(as3(o), bc3(i), AF.Identity),
                         waits=[t1], inc=("act", 1))
            P.emit("vector", lambda e, o=e1b[:], a=lgv, b=mb1[:]:
                   e.tensor_tensor(o, a, b, ALU.is_equal), waits=[t2_])
            P.emit("vector", lambda e, o=l2b[:], i=e1b[:], j=lgv:
                   e.scalar_tensor_tensor(o, i, -1e30, j, ALU.mult, ALU.add))
            t5 = P.emit("vector", lambda e, o=tkv[:, 32:64], i=l2b[:]:
                        e.reduce_max(o.rearrange("p (c e) -> p c e", e=1),
                                     as3(i), AX.X), inc=("dve", 1))
            t6 = P.emit("scalar", lambda e, o=mb2[:], i=tkv[:, 32:64]:
                        e.activation(as3(o), bc3(i), AF.Identity),
                        waits=[t5], inc=("act", 1))
            P.emit("vector", lambda e, o=l2b[:], a=l2b[:], b=mb2[:]:
                   e.tensor_tensor(o, a, b, ALU.is_equal), waits=[t6])
            t8 = P.emit("vector", lambda e, o=e1b[:], a=e1b[:], b=l2b[:]:
                        e.tensor_add(o, a, b), inc=("dve", 1))
            t9 = P.emit("scalar", lambda e, o=pbb[:], i=lgv:
                        e.activation(o, i, AF.Sigmoid), waits=[t8],
                        inc=("act", 1))
            P.emit("scalar", lambda e, o=tkv[:, 64:96], i=tkv[:, 0:32]:
                   e.activation(o, i, AF.Sigmoid))
            t11 = P.emit("scalar", lambda e, o=tkv[:, 96:128], i=tkv[:, 32:64]:
                         e.activation(o, i, AF.Sigmoid), inc=("act", 1))
            t12 = P.emit("vector", lambda e, o=tkv[:, 0:32], a=tkv[:, 64:96],
                         b=tkv[:, 96:128]: e.tensor_add(o, a, b),
                         waits=[t11], inc=("dve", 1))
            t13 = P.emit("scalar",
                         lambda e, o=tkv[:, 32:64], i=tkv[:, 0:32],
                         b=cols[:, 46:47]:
                         e.activation(o, i, AF.Identity, bias=b),
                         waits=[t12], inc=("act", 1))
            t14 = P.emit("vector", lambda e, o=tkv[:, 64:96], i=tkv[:, 32:64]:
                         e.reciprocal(o, i), waits=[t13], inc=("dve", 1))
            t15 = P.emit("scalar", lambda e, o=mb3[:], i=tkv[:, 64:96]:
                         e.activation(as3(o), bc3(i), AF.Identity),
                         waits=[t14], inc=("act", 1))
            P.emit("vector", lambda e, o=pbb[:], a=pbb[:], b=e1b[:]:
                   e.tensor_mul(o, a, b), waits=[t9])
            P.emit("vector", lambda e, o=comb[:], a=pbb[:], b=mb3[:]:
                   e.tensor_mul(o, a, b), waits=[t15], inc=("dve", 1))
            if "comb_s" in dbg:
                cdb = nc.dram_tensor("comb_s", [128, 256], F32,
                                     kind="ExternalOutput").ap()
                ldb = nc.dram_tensor("log_s", [128, 256], F32,
                                     kind="ExternalOutput").ap()
                P.dma("sync", cdb, comb[:], "sdbg", waits=P.cur_all())
                P.dma("sync", ldb, logits[:], "sdbg")
            if "qk_s" in dbg:
                qk_dbg = nc.dram_tensor("qk_s", [2, 128, T], BF16,
                                        kind="ExternalOutput").ap()
                P.dma("sync", qk_dbg[0], q_fm[:], "sdbg", waits=P.cur_all())
                P.dma("sync", qk_dbg[1], k_fm[:], "sdbg")
            P.join_all()
            c0.close()
            cA.close()

        if "p3" in phases:
            with ExitStack() as c3:
                mbias = sb("mbias_b", [128, MW], ctx=c3)
                kernT = sb("kernT", [128, 32 * 512], BF16, ctx=c3)
                owt3 = sb("owt3", [128, 16 * 1024], BF16, ctx=c3)
                u_sl = sb("u_sl", [128, 16 * 512], BF16, ctx=c3)
                outp = sb("outp", [128, 16 * 512], BF16, ctx=c3)
                vbuf = [sb(f"vbuf{i}", [128, 4 * 512], BF16, ctx=c3) for i in range(2)]
                t1b = [sb(f"t1b{i}", [128, 512], ctx=c3) for i in range(2)]
                rb2 = [sb(f"rb2{i}", [128, 512], ctx=c3) for i in range(2)]
                prj = [sb(f"prj{i}", [128, 512], ctx=c3) for i in range(2)]
                at2a = [sb(f"at2a{i}", [128, 4 * 1024], BF16, ctx=c3) for i in range(2)]

                mb_tok = P.dma("sync", mbias[:], mbias_h, "s_mb")
                owt_tok = P.dma("sync",
                                owt3[:].rearrange("p (c o) -> p c o", o=1024),
                                owTb.rearrange("(c p) o -> p c o", p=128), "sowt",
                                waits=[owT_cast])
                sui_tok = None
                usl_rd = None
                vb_tok = [None, None]
                vb_rd = [None, None]
                at_tok = [None, None]
                outp_rd = None
                kern_tok = [None] * 32
                rb_rd = [None, None]

                for qt in range(8):
                    q0 = qt * 512
                    sui_tok = P.dma(
                        "sync",
                        u_sl[:].rearrange("p (c t) -> p c t", t=512),
                        u_hbm[:, :, q0:q0 + 512].rearrange("c p t -> p c t"),
                        "sui", waits=[usl_rd])
                    for kt in range(32):
                        bk = 4 + kt % 2
                        off = 3968 - kt * 128 + q0
                        smm = P.emit("tensor",
                                     mmf(PS[bk][:, 0:512], k_fm[:, kt * 128:(kt + 1) * 128],
                                         q_fm[:, q0:q0 + 512], True, True),
                                     waits=[ps_rd[bk]], inc=("pe", 1))
                        ps_wr[bk] = smm
                        a1 = P.emit("vector",
                                    lambda e, o=t1b[kt % 2][:], a=PS[bk][:, 0:512],
                                    b=mbias[:, off:off + 512]: e.tensor_add(o, a, b),
                                    waits=[smm, mb_tok], inc=("dve", 1))
                        ps_rd[bk] = a1
                        a2 = P.emit("scalar",
                                    lambda e, o=rb2[kt % 2][:], i=t1b[kt % 2][:]:
                                    e.activation(o, i, AF.Relu),
                                    waits=[a1, rb_rd[kt % 2]], inc=("act", 1))
                        a3 = P.emit("vector",
                                    lambda e, o=kernT[:, kt * 512:(kt + 1) * 512],
                                    a=t1b[kt % 2][:], b=rb2[kt % 2][:]:
                                    e.tensor_mul(o, a, b),
                                    waits=[a2], inc=("dve", 1))
                        rb_rd[kt % 2] = a3
                        kern_tok[kt] = a3
                    for g in range(4):
                        gtok = None
                        for kt in range(32):
                            kb = kt // 4
                            if kt % 4 == 0:
                                vb_tok[kb % 2] = P.dma(
                                    "scalar",
                                    vbuf[kb % 2][:].rearrange(
                                        "p (k e) -> p k e", e=512),
                                    v_hbm[kt:kt + 4, :, g * 512:(g + 1) * 512]
                                    .rearrange("k p e -> p k e"),
                                    f"svi{kb % 2}", waits=[vb_rd[kb % 2]])
                            vsl = vbuf[kb % 2][:, (kt % 4) * 512:(kt % 4) * 512 + 512]
                            for ec4 in range(4):
                                gtok = P.emit(
                                    "tensor",
                                    mmf(PS[ec4][:, 0:512],
                                        vsl[:, ec4 * 128:(ec4 + 1) * 128],
                                        kernT[:, kt * 512:(kt + 1) * 512],
                                        kt == 0, kt == 31, skip=True),
                                    waits=[vb_tok[kb % 2], kern_tok[kt], ps_rd[ec4]],
                                    inc=("pe", 1) if ec4 == 3 else None)
                            if kt % 4 == 3:
                                vb_rd[kb % 2] = gtok
                        for ec4 in range(4):
                            ec = g * 4 + ec4
                            ev = P.emit(
                                "vector",
                                lambda e, o=outp[:, ec * 512:(ec + 1) * 512],
                                a=PS[ec4][:, 0:512], b=u_sl[:, ec * 512:(ec + 1) * 512]:
                                e.tensor_mul(o, a, b),
                                waits=[gtok, sui_tok, outp_rd], inc=("dve", 1))
                            ps_rd[ec4] = ev
                    usl_rd = ps_rd[3]
                    ev_out = ps_rd[3]
                    prj_rd = [None, None]
                    tev = None
                    for hh in range(8):
                        bk = 4 + hh % 2
                        pmm = None
                        for ec in range(16):
                            pmm = P.emit(
                                "tensor",
                                mmf(PS[bk][:, 0:512],
                                    owt3[:, ec * 1024 + hh * 128:ec * 1024 + (hh + 1) * 128],
                                    outp[:, ec * 512:(ec + 1) * 512],
                                    ec == 0, ec == 15),
                                waits=[owt_tok, ev_out, ps_rd[bk]],
                                inc=("pe", 1) if ec == 15 else None)
                        ps_wr[bk] = pmm
                        outp_rd = pmm
                        pe1 = P.emit("scalar",
                                     lambda e, o=prj[hh % 2][:], i=PS[bk][:, 0:512],
                                     b=cols[:, 33 + hh:34 + hh]:
                                     e.activation(o, i, AF.Identity, bias=b),
                                     waits=[pmm, prj_rd[hh % 2]], inc=("act", 1))
                        ps_rd[bk] = pe1
                        for tc4 in range(4):
                            bkT = 6 + tc4 % 2
                            ttr = P.emit("tensor",
                                         lambda e, o=PS[bkT][:, 0:128],
                                         i=prj[hh % 2][:, tc4 * 128:(tc4 + 1) * 128],
                                         d=identf[:]: e.transpose(o, i, d),
                                         waits=[pe1, ps_rd[bkT]], inc=("pe", 1))
                            ps_wr[bkT] = ttr
                            prj_rd[hh % 2] = ttr
                            tev = P.emit(
                                "vector",
                                vcopy(at2a[qt % 2][:, tc4 * 1024 + hh * 128:
                                                   tc4 * 1024 + (hh + 1) * 128],
                                      PS[bkT][:, 0:128]),
                                waits=[ttr, at_tok[qt % 2]], inc=("dve", 1))
                            ps_rd[bkT] = tev
                    at_tok[qt % 2] = P.dma(
                        "sync",
                        at2_hbm[q0:q0 + 512, :].rearrange("(c p) o -> p c o", p=128),
                        at2a[qt % 2][:].rearrange("p (c o) -> p c o", o=1024),
                        f"sat{qt % 2}", waits=[tev])
                P.join_all()

        if "ffn" in phases:
            with ExitStack() as cF:
                xn_sl = [sb(f"xnsl{i}", [128, 8 * 512], BF16, ctx=cF) for i in range(2)]
                h_sb = sb("h_sb", [128, 32 * 512], BF16, ctx=cF)
                rbuf = [sb(f"rbuf{i}", [128, 512], ctx=cF) for i in range(2)]
                wu_sl = [sb(f"wusl{i}", [128, 8 * 512], BF16, ctx=cF) for i in range(2)]
                wd_sl = [sb(f"wdsl{i}", [128, 4 * 1024], BF16, ctx=cF) for i in range(2)]
                acc = [sb(f"acc{i}", [128, 4 * 1024], ctx=cF) for i in range(2)]
                r0f = sb("r0f", [128, 4 * 1024], ctx=cF)
                a2f = sb("a2f", [128, 4 * 1024], BF16, ctx=cF)

                xn_tok = [None, None]
                xn_rd = [None, None]
                wu_tok = [None, None]
                wu_rd = [None, None]
                wd_tok = [None, None]
                wd_rd = [None, None]
                st_tok = [None, None]
                rb_rd = [None, None]
                r0_rd = None
                a2_rd = None
                h_rd = None

                for tt in range(8):
                    t0 = tt * 512
                    xn_tok[tt % 2] = P.dma(
                        "sync",
                        xn_sl[tt % 2][:].rearrange("p (c t) -> p c t", t=512),
                        xn_hbm[:, :, t0:t0 + 512].rearrange("c p t -> p c t"),
                        f"sxi{tt % 2}", waits=[xn_rd[tt % 2]])
                    r0_tok = P.dma(
                        "sync",
                        r0f[:].rearrange("p (c o) -> p c o", o=1024),
                        r0_hbm[t0:t0 + 512, :].rearrange("(c p) o -> p c o", p=128),
                        "sri", waits=[r0_rd])
                    a2_tok = P.dma(
                        "sync",
                        a2f[:].rearrange("p (c o) -> p c o", o=1024),
                        at2_hbm[t0:t0 + 512, :].rearrange("(c p) o -> p c o", p=128),
                        "sai", waits=[a2_rd])
                    for ei in range(9):
                        e_idx = None if ei == 0 else ei - 1
                        uh_tok = [None] * 32
                        for js in range(8):
                            sl = js % 2
                            j0 = js * 512
                            if e_idx is None:
                                src = wsub[:, j0:j0 + 512]
                                ctok = sh_cast
                            else:
                                src = wupb[e_idx, :, j0:j0 + 512]
                                ctok = exp_cast[e_idx]
                            wu_tok[sl] = P.dma(
                                "sync",
                                wu_sl[sl][:].rearrange("p (c j) -> p c j", j=512),
                                src.rearrange("(c p) j -> p c j", p=128),
                                f"swu{sl}", waits=[wu_rd[sl], ctok])
                            for jb in range(4):
                                j = js * 4 + jb
                                bk = j % 2
                                umt = None
                                for dc in range(DC):
                                    umt = P.emit(
                                        "tensor",
                                        mmf(PS[bk][:, 0:512],
                                            wu_sl[sl][:, dc * 512 + jb * 128:
                                                       dc * 512 + (jb + 1) * 128],
                                            xn_sl[tt % 2][:, dc * 512:(dc + 1) * 512],
                                            dc == 0, dc == 7),
                                        waits=[wu_tok[sl], xn_tok[tt % 2], ps_rd[bk]],
                                        inc=("pe", 1) if dc == 7 else None)
                                ps_wr[bk] = umt
                                wu_rd[sl] = umt
                                ur = P.emit("scalar",
                                            lambda e, o=rbuf[bk][:], i=PS[bk][:, 0:512]:
                                            e.activation(o, i, AF.Relu),
                                            waits=[umt, rb_rd[bk]], inc=("act", 1))
                                ps_rd[bk] = ur
                                uh = P.emit("vector",
                                            lambda e, o=h_sb[:, j * 512:(j + 1) * 512],
                                            a=rbuf[bk][:], b=rbuf[bk][:]:
                                            e.tensor_mul(o, a, b),
                                            waits=[ur, h_rd], inc=("dve", 1))
                                rb_rd[bk] = uh
                                uh_tok[j] = uh
                        xn_rd[tt % 2] = wu_rd[1] if wu_rd[1] else wu_rd[0]
                        dmt = None
                        for js in range(8):
                            sl = js % 2
                            j0 = js * 512
                            if e_idx is None:
                                srcd = wsdb[j0:j0 + 512, :]
                                ctok = sh_cast
                            else:
                                srcd = wdnb[e_idx, j0:j0 + 512, :]
                                ctok = exp_cast[e_idx]
                            wd_tok[sl] = P.dma(
                                "scalar",
                                wd_sl[sl][:].rearrange("p (c o) -> p c o", o=1024),
                                srcd.rearrange("(c p) o -> p c o", p=128),
                                f"swd{sl}", waits=[wd_rd[sl], ctok])
                            for jc4 in range(4):
                                j = js * 4 + jc4
                                for tc4 in range(4):
                                    for os2 in range(2):
                                        g = tc4 * 2 + os2
                                        dmt = P.emit(
                                            "tensor",
                                            mmf(PS[g][:, 0:512],
                                                h_sb[:, j * 512 + tc4 * 128:
                                                     j * 512 + (tc4 + 1) * 128],
                                                wd_sl[sl][:, jc4 * 1024 + os2 * 512:
                                                          jc4 * 1024 + (os2 + 1) * 512],
                                                js == 0 and jc4 == 0,
                                                js == 7 and jc4 == 3, skip=True),
                                            waits=[wd_tok[sl], uh_tok[j], ps_rd[g]],
                                            inc=("pe", 1) if (jc4 == 3 and g == 7)
                                            else None)
                                wd_rd[sl] = dmt if dmt is not None else wd_rd[sl]
                        h_rd = dmt
                        for tc4 in range(4):
                            for os2 in range(2):
                                g = tc4 * 2 + os2
                                dst = acc[tt % 2][:, tc4 * 1024 + os2 * 512:
                                                  tc4 * 1024 + (os2 + 1) * 512]
                                if e_idx is None:
                                    evt = P.emit("vector", vcopy(dst, PS[g][:, 0:512]),
                                                 waits=[dmt, st_tok[tt % 2]],
                                                 inc=("dve", 1))
                                else:
                                    cidx = (tt * 4 + tc4) * 8 + e_idx
                                    evt = P.emit(
                                        "vector",
                                        lambda e, o=dst, i=PS[g][:, 0:512],
                                        s=comb[:, cidx:cidx + 1]:
                                        e.scalar_tensor_tensor(o, i, s, o,
                                                               ALU.mult, ALU.add),
                                        waits=[dmt], inc=("dve", 1))
                                ps_rd[g] = evt
                    f2 = None
                    for tc4 in range(4):
                        dst = acc[tt % 2][:, tc4 * 1024:(tc4 + 1) * 1024]
                        P.emit("vector", lambda e, o=dst, a=dst,
                               b=r0f[:, tc4 * 1024:(tc4 + 1) * 1024]:
                               e.tensor_add(o, a, b), waits=[r0_tok])
                        f2 = P.emit("vector", lambda e, o=dst, a=dst,
                                    b=a2f[:, tc4 * 1024:(tc4 + 1) * 1024]:
                                    e.tensor_add(o, a, b), waits=[a2_tok],
                                    inc=("dve", 1))
                    r0_rd = f2
                    a2_rd = f2
                    st_tok[tt % 2] = P.dma(
                        "sync",
                        out_h[t0:t0 + 512, :].rearrange("(c p) o -> p c o", p=128),
                        acc[tt % 2][:].rearrange("p (c o) -> p c o", o=1024),
                        f"sst{tt % 2}", waits=[f2])
        P.wait("sync", P.cur_all())

        P.build()
    return nc, P


def _host_tables(ins):
    import jax
    import jax.numpy as jnp

    w = np.asarray(ins["L0_attn_w"], np.float32)
    raw = np.arange(128)[:, None] - np.arange(MW)[None, :] + 4479
    idx = np.clip(np.where(raw < 0, raw + 1023, raw), 0, 1022)
    M = w[idx].astype(np.float32)

    with jax.default_device(jax.devices("cpu")[0]):
        pos_j = jnp.arange(T, dtype=jnp.float32)
        inv_j = 10000.0 ** (jnp.arange(64, dtype=jnp.float32) / 64)
        sinus = pos_j[:, None] * inv_j[None, :]
        sin_t = np.asarray(jnp.sin(sinus))
        cos_t = np.asarray(jnp.cos(sinus))
    cos_dup = np.ascontiguousarray(np.concatenate([cos_t, cos_t], 1).T)
    sinsgn = np.ascontiguousarray(np.concatenate([-sin_t, sin_t], 1).T)
    SQ = np.float32(1.0 / np.sqrt(128.0))

    cols = np.zeros((128, 64), np.float32)
    cols[:, 0:33] = np.asarray(ins["L0_attn_uv_b"], np.float32).reshape(33, 128).T
    cols[:, 33:41] = np.asarray(ins["L0_attn_o_b"], np.float32).reshape(8, 128).T
    g = np.asarray(ins["L0_attn_gamma"], np.float32)
    b = np.asarray(ins["L0_attn_beta"], np.float32)
    cols[:, 41] = g[0]
    cols[:, 42] = g[1]
    cols[:, 43] = b[0]
    cols[:, 44] = b[1]
    cols[:, 45] = EPS
    cols[:, 46] = 1e-6
    cols[:, 47] = -0.5
    cols[:, 48] = 1.0 / D

    permm = np.zeros((128, 128), np.float32)
    permm[(np.arange(128) + 64) % 128, np.arange(128)] = 1.0

    return {
        "uvwT": np.ascontiguousarray(np.asarray(ins["L0_attn_uv_w"], np.float32).T),
        "owT": np.ascontiguousarray(np.asarray(ins["L0_attn_o_w"], np.float32).T),
        "wr": np.ascontiguousarray(np.asarray(ins["L0_ffn_W_router"], np.float32)),
        "wup": np.ascontiguousarray(np.asarray(ins["L0_ffn_W_up"], np.float32)),
        "wdn": np.ascontiguousarray(np.asarray(ins["L0_ffn_W_down"], np.float32)),
        "wsu": np.ascontiguousarray(np.asarray(ins["L0_ffn_W_sh_up"], np.float32)),
        "wsd": np.ascontiguousarray(np.asarray(ins["L0_ffn_W_sh_down"], np.float32)),
        "cols": cols,
        "lnw": np.ascontiguousarray(
            np.broadcast_to(np.asarray(ins["L0_attn_ln_w"], np.float32), (128, D))),
        "lnb": np.ascontiguousarray(
            np.broadcast_to(np.asarray(ins["L0_attn_ln_b"], np.float32), (128, D))),
        "mbias": M,
        "qcos": np.ascontiguousarray(cos_dup * SQ),
        "qsin": np.ascontiguousarray(sinsgn * SQ),
        "kcos": cos_dup,
        "ksin": sinsgn,
        "uvbrow_f": np.asarray(ins["L0_attn_uv_b"], np.float32).reshape(1, 4224).copy(),
        "ones1": np.ones((1, 128), dtype=ml_dtypes.bfloat16),
        "perm": permm.astype(ml_dtypes.bfloat16),
        "identf": np.eye(128, dtype=np.float32),
    }


def kernel(**ins):
    tabs = _host_tables(ins)
    x = np.asarray(ins["L0_attn_x"], np.float32)
    nc, _ = build_nc()
    in_maps = []
    for c in range(8):
        m = dict(tabs)
        m["x"] = np.ascontiguousarray(x[c])
        in_maps.append(m)
    res = run_bass_kernel_spmd(nc, in_maps, core_ids=list(range(8)))
    return np.stack([res.results[c]["out"] for c in range(8)], axis=0)


# version 13
# speedup vs baseline: 1.1766x; 1.1766x over previous
import numpy as np
import ml_dtypes
from contextlib import ExitStack

import concourse.bass as bass
import concourse.mybir as mybir
from concourse.bass_utils import run_bass_kernel_spmd

F32 = mybir.dt.float32
BF16 = mybir.dt.bfloat16
AF = mybir.ActivationFunctionType
ALU = mybir.AluOpType
AX = mybir.AxisListType

T = 4096
D = 1024
DC = D // 128
EUV = 2048
H = 4096
EPS = 1e-5
MW = 8064


class Prog:

    ENGS = ("sync", "scalar", "vector", "tensor", "gpsimd")

    def __init__(self, nc):
        self.nc = nc
        self.ops = {e: [] for e in self.ENGS}
        self.cnt = {}
        self.hw = {}
        self.sems = {}

    def _sem(self, name):
        if name not in self.sems:
            self.sems[name] = self.nc.alloc_semaphore(name)
            self.cnt[name] = 0
        return self.sems[name]

    def emit(self, eng, fn, waits=(), inc=None):
        best = {}
        for wt in waits:
            if wt is None:
                continue
            s, v = wt[0], wt[1]
            peng = wt[2] if len(wt) > 2 else None
            if v <= 0 or peng == eng:
                continue
            if best.get(s, -1) < v:
                best[s] = v
        wl = []
        for s, v in best.items():
            k = (eng, s)
            if self.hw.get(k, 0) >= v:
                continue
            self.hw[k] = v
            wl.append((s, v))
        tok = None
        if inc is not None:
            s, amt = inc
            self._sem(s)
            self.cnt[s] += amt
            tok = (s, self.cnt[s], eng if amt == 1 else None)
        self.ops[eng].append((wl, fn, inc))
        return tok

    def dma(self, eng, out, in_, sem_name, waits=()):
        self._sem(sem_name)
        return self.emit(eng, (lambda e, o=out, i=in_: e.dma_start(out=o, in_=i)),
                         waits=waits, inc=(sem_name, 16))

    def wait(self, eng, waits):
        self.emit(eng, None, waits=waits)

    def cur_all(self):
        return [(s, c, None) for s, c in self.cnt.items() if c > 0]

    def join_all(self):
        toks = self.cur_all()
        for e in self.ENGS:
            self.wait(e, toks)

    def build(self):
        with self.nc.Block() as block:
            for eng in self.ENGS:
                ops = self.ops[eng]
                if not ops:
                    continue
                getattr(block, eng)(lambda e, ops=ops: self._replay(e, ops))

    def _replay(self, e, ops):
        for wl, fn, inc in ops:
            for (s, v) in wl:
                e.wait_ge(self.sems[s], v)
            if fn is not None:
                inst = fn(e)
                if inc is not None:
                    inst.then_inc(self.sems[inc[0]], inc[1])

    def n_ops(self):
        return sum(len(v) for v in self.ops.values())


def build_nc(phases=("p0", "p1", "p3", "ffn"), dbg=(), serial=False):
    nc = bass.Bass("TRN2", target_bir_lowering=False)

    def din(name, shape, dt=F32):
        return nc.dram_tensor(name, shape, dt, kind="ExternalInput").ap()

    x_h = din("x", [T, D])
    uvwT_h = din("uvwT", [D, 4224])
    owT_h = din("owT", [EUV, D])
    wr_h = din("wr", [D, 8])
    wup_h = din("wup", [8, D, H])
    wdn_h = din("wdn", [8, H, D])
    wsu_h = din("wsu", [D, H])
    wsd_h = din("wsd", [H, D])
    cols_h = din("cols", [128, 64])
    lnw_h = din("lnw", [128, D])
    lnb_h = din("lnb", [128, D])
    mbias_h = din("mbias", [128, MW])
    qcos_h = din("qcos", [128, T])
    qsin_h = din("qsin", [128, T])
    kcos_h = din("kcos", [128, T])
    ksin_h = din("ksin", [128, T])
    uvbrow_h = din("uvbrow_f", [1, 4224])
    ones1_h = din("ones1", [1, 128], BF16)
    perm_h = din("perm", [128, 128], BF16)
    identf_h = din("identf", [128, 128])

    out_h = nc.dram_tensor("out", [T, D], F32, kind="ExternalOutput").ap()

    def scratch(name, shape, dt):
        kind = "ExternalOutput" if name in dbg else "Internal"
        return nc.dram_tensor(name, shape, dt, kind=kind).ap()

    uvwTb = nc.dram_tensor("uvwTb", [D, 4224], BF16, kind="Internal").ap()
    owTb = nc.dram_tensor("owTb", [EUV, D], BF16, kind="Internal").ap()
    wupb = nc.dram_tensor("wupb", [8, D, H], BF16, kind="Internal").ap()
    wdnb = nc.dram_tensor("wdnb", [8, H, D], BF16, kind="Internal").ap()
    wsub = nc.dram_tensor("wsub", [D, H], BF16, kind="Internal").ap()
    wsdb = nc.dram_tensor("wsdb", [H, D], BF16, kind="Internal").ap()

    xn_hbm = scratch("xn_s", [DC, 128, T], BF16)
    xntm_hbm = scratch("xntm_s", [T, D], F32) if "xntm_s" in dbg else None
    r0_hbm = scratch("r0_s", [T, D], F32)
    at2_hbm = scratch("at2_s", [T, D], BF16)
    u_hbm = scratch("u_s", [16, 128, T], BF16)
    v_hbm = scratch("v_s", [32, 128, EUV], BF16)

    P = Prog(nc)
    RI = float(1.0 / D)

    with ExitStack() as top:
        def sb(name, shape, dt=F32, ctx=None):
            return (ctx or top).enter_context(nc.sbuf_tensor(name, shape, dt))

        cols = sb("cols_b", [128, 64])
        identf = sb("identf_b", [128, 128])
        perm = sb("perm_b", [128, 128], BF16)
        wr_sb = sb("wr_sb", [128, 64])
        logits = sb("logits", [128, 32 * 8])
        comb = sb("comb", [128, 32 * 8])
        q_fm = sb("q_fm", [128, T], BF16)
        k_fm = sb("k_fm", [128, T], BF16)
        PS = [top.enter_context(nc.psum_tensor(f"ps{i}", [128, 512], F32))
              for i in range(8)]
        ps_wr = [None] * 8
        ps_rd = [None] * 8

        P.dma("sync", cols[:], cols_h, "s_const")
        P.dma("sync", identf[:], identf_h, "s_const")
        P.dma("sync", perm[:], perm_h, "s_const")
        P.dma("sync", wr_sb[:].rearrange("p (c e) -> p c e", e=8),
              wr_h.rearrange("(c p) e -> p c e", p=128), "s_const")

        def mmf(psap, lh, rh, st, sp, skip=False):
            return (lambda e, a=psap, b=lh, c=rh, s=st, p=sp, k=skip:
                    e.matmul(a, b, c, start=s, stop=p, skip_group_check=k))

        def vcopy(o, i):
            return (lambda e, a=o, b=i: e.tensor_copy(out=a, in_=b))

        def scopy(o, i):
            return (lambda e, a=o, b=i: e.copy(out=a, in_=b))

        cA = top.enter_context(ExitStack())
        xn2_fm = sb("xn2_fm", [128, DC * T], BF16, ctx=cA)
        do_p1 = "p1" in phases
        if do_p1:
            slab = [sb(f"slab{i}", [128, 8 * 512], BF16, ctx=cA) for i in range(2)]
            ubuf = [sb(f"ubuf{i}", [128, 2048], BF16, ctx=cA) for i in range(2)]
            vasm = [sb(f"vasm{i}", [128, 2048], BF16, ctx=cA) for i in range(2)]
            ropeb = [sb("ropeb0", [128, 4 * 512], ctx=cA)]
            ropeb.append(ropeb[0])
            uvbrow = sb("uvbrow_b", [1, 4224], BF16, ctx=cA)
            uvbrow_lo = sb("uvbrow_lo", [1, 4224], BF16, ctx=cA)
            ones1 = sb("ones1_b", [1, 128], BF16, ctx=cA)
            bsf = [sb(f"bsf{i}", [128, 512], ctx=cA) for i in range(2)]
            qpf = sb("qpf", [128, 512], ctx=cA)
            kpf = sb("kpf", [128, 512], ctx=cA)
            qpb = sb("qpb", [128, 512], BF16, ctx=cA)
            kpb = sb("kpb", [128, 512], BF16, ctx=cA)
            tm1 = sb("tm1", [128, 512], ctx=cA)
            tm2 = sb("tm2", [128, 512], ctx=cA)

        hi_t = None
        if do_p1:
            hi_t = P.dma("gpsimd", uvbrow[:], uvbrow_h, "s_bhi")
        uvwT_cast = P.dma("gpsimd", uvwTb, uvwT_h, "swc")
        owT_cast = P.dma("gpsimd", owTb, owT_h, "swc", waits=[uvwT_cast])
        sh_cast = P.dma("gpsimd", wsub, wsu_h, "swc", waits=[owT_cast])
        sh_cast = P.dma("gpsimd", wsdb, wsd_h, "swc", waits=[sh_cast])

        exp_cast = []

        def emit_expert_casts(gate_tok):
            prev = sh_cast
            for e_ in range(8):
                prev = P.dma("gpsimd", wupb[e_], wup_h[e_], "swc",
                             waits=[prev, gate_tok])
                gate_tok = None
                prev = P.dma("gpsimd", wdnb[e_], wdn_h[e_], "swc",
                             waits=[prev])
                exp_cast.append(prev)

        if "p0" in phases:
            c0 = top.enter_context(ExitStack())
            lnw_bc = sb("lnw_bc", [128, D], ctx=c0)
            lnb_bc = sb("lnb_bc", [128, D], ctx=c0)
            xt = [sb(f"xt{i}", [128, D], ctx=c0) for i in range(2)]
            xnf = [sb(f"xnf{i}", [128, D], ctx=c0) for i in range(2)]
            sqb = sb("sqb", [128, D], ctx=c0)
            t2 = sb("t2", [128, D], ctx=c0)
            x2f = sb("x2f", [128, D], ctx=c0)
            xfm32 = sb("xfm32", [128, D], ctx=c0)
            stg = [sb(f"stg{i}", [128, D], BF16, ctx=c0) for i in range(2)]
            r0b = [sb(f"r0b{i}", [128, D], ctx=c0) for i in range(2)]
            scl = sb("scl", [128, 16], ctx=c0)

            P.dma("sync", lnw_bc[:], lnw_h, "s_const")
            P.dma("sync", lnb_bc[:], lnb_h, "s_const")
            const_tok = ("s_const", P.cnt["s_const"], None)
            eps_col = cols[:, 45:46]
            ri_col = cols[:, 48:49]
            nh_col = cols[:, 47:48]

            sx_tok = [None, None]
            sr0_tok = [None, None]
            sxn_tok = [None, None]
            xt_rd = [None, None]
            xnf_rd = [None, None]
            x2f_rd = None
            xfm_rd = None
            xn2_tok = [None] * 32

            def emit_p0_chunk(tc):
                nonlocal x2f_rd, xfm_rd
                p = tc % 2
                t0 = tc * 128
                sx_tok[p] = P.dma("sync", xt[p][:], x_h[t0:t0 + 128, :],
                                  f"sx{p}", waits=[xt_rd[p], sr0_tok[p]])
                w0 = [sx_tok[p], const_tok]
                ts1 = P.emit("vector", lambda e, a=scl[:, 0:1], b=xt[p][:]:
                             e.reduce_sum(a, b, AX.X), waits=w0,
                             inc=("dve", 1))
                tmu = P.emit("scalar",
                             lambda e, o=scl[:, 1:2], i=scl[:, 0:1], s=ri_col:
                             e.activation(o, i, AF.Identity, scale=s),
                             waits=[ts1], inc=("act", 1))
                P.emit("vector", lambda e, a=xt[p][:], s=scl[:, 1:2]:
                       e.tensor_scalar(a, a, s, None, ALU.subtract),
                       waits=[tmu])
                P.emit("vector", lambda e, o=sqb[:], i=xt[p][:]:
                       e.tensor_mul(o, i, i))
                ts2 = P.emit("vector", lambda e, o=scl[:, 2:3], i=sqb[:]:
                             e.reduce_sum(o, i, AX.X), inc=("dve", 1))
                P.emit("scalar",
                       lambda e, o=scl[:, 3:4], i=scl[:, 2:3], s=ri_col,
                       b=eps_col: e.activation(o, i, AF.Ln, scale=s, bias=b),
                       waits=[ts2])
                trs = P.emit("scalar",
                             lambda e, o=scl[:, 4:5], i=scl[:, 3:4], s=nh_col:
                             e.activation(o, i, AF.Exp, scale=s),
                             inc=("act", 1))
                P.emit("vector", lambda e, o=xnf[p][:], i=xt[p][:], s=scl[:, 4:5]:
                       e.tensor_scalar_mul(o, i, s), waits=[xnf_rd[p], trs])
                xt_rd[p] = P.emit(
                    "vector",
                    lambda e, o=r0b[p][:], i=xt[p][:], s=scl[:, 1:2], j=xnf[p][:]:
                    e.scalar_tensor_tensor(o, i, s, j, ALU.add, ALU.add),
                    waits=[sr0_tok[p]], inc=("dve", 1))
                sr0_tok[p] = P.dma("sync", r0_hbm[t0:t0 + 128, :], r0b[p][:],
                                   f"sr0{p}", waits=[xt_rd[p]])
                ts3 = P.emit("vector", lambda e, a=scl[:, 6:7], b=xnf[p][:]:
                             e.reduce_sum(a, b, AX.X), inc=("dve", 1))
                tmu2 = P.emit("scalar",
                              lambda e, o=scl[:, 7:8], i=scl[:, 6:7], s=ri_col:
                              e.activation(o, i, AF.Identity, scale=s),
                              waits=[ts3], inc=("act", 1))
                P.emit("vector", lambda e, o=t2[:], i=xnf[p][:], s=scl[:, 7:8]:
                       e.tensor_scalar(o, i, s, None, ALU.subtract),
                       waits=[x2f_rd, tmu2])
                P.emit("vector", lambda e, o=sqb[:], i=t2[:]:
                       e.tensor_mul(o, i, i))
                ts4 = P.emit("vector", lambda e, o=scl[:, 8:9], i=sqb[:]:
                             e.reduce_sum(o, i, AX.X), inc=("dve", 1))
                P.emit("scalar",
                       lambda e, o=scl[:, 9:10], i=scl[:, 8:9], s=ri_col,
                       b=eps_col: e.activation(o, i, AF.Ln, scale=s, bias=b),
                       waits=[ts4])
                trs2 = P.emit("scalar",
                              lambda e, o=scl[:, 10:11], i=scl[:, 9:10], s=nh_col:
                              e.activation(o, i, AF.Exp, scale=s),
                              inc=("act", 1))
                P.emit("vector", lambda e, a=t2[:], s=scl[:, 10:11]:
                       e.tensor_scalar_mul(a, a, s), waits=[trs2])
                P.emit("vector", lambda e, a=t2[:], b=lnw_bc[:]:
                       e.tensor_mul(a, a, b))
                tx2f = P.emit("vector", lambda e, o=x2f[:], a=t2[:], b=lnb_bc[:]:
                              e.tensor_add(o, a, b), inc=("dve", 1))

                ev_last = None
                for c8 in range(8):
                    bk = c8 % 2
                    ttr = P.emit("tensor",
                                 lambda e, o=PS[bk][:, 0:128],
                                 i=xnf[p][:, c8 * 128:(c8 + 1) * 128], d=identf[:]:
                                 e.transpose(o, i, d),
                                 waits=[ps_rd[bk], const_tok, xt_rd[p]],
                                 inc=("pe", 1))
                    ps_wr[bk] = ttr
                    P.emit("scalar", scopy(stg[p][:, c8 * 128:(c8 + 1) * 128],
                                           PS[bk][:, 0:128]),
                           waits=[ttr, sxn_tok[p]])
                    ev_last = P.emit("scalar",
                                     scopy(xfm32[:, c8 * 128:(c8 + 1) * 128],
                                           PS[bk][:, 0:128]),
                                     waits=[xfm_rd], inc=("act", 1))
                    ps_rd[bk] = ev_last
                xnf_rd[p] = ev_last
                sxn_tok[p] = P.dma(
                    "sync",
                    xn_hbm[:, :, t0:t0 + 128].rearrange("c p t -> p c t"),
                    stg[p][:].rearrange("p (c t) -> p c t", t=128),
                    f"sxn{p}", waits=[ev_last])
                rmm = None
                for c8 in range(8):
                    rmm = P.emit("tensor",
                                 mmf(PS[4][:, 0:8], xfm32[:, c8 * 128:(c8 + 1) * 128],
                                     wr_sb[:, c8 * 8:(c8 + 1) * 8],
                                     c8 == 0, c8 == 7),
                                 waits=[ev_last, ps_rd[4], const_tok],
                                 inc=("pe", 1))
                ps_wr[4] = rmm
                xfm_rd = rmm
                lgt = P.emit("vector", vcopy(logits[:, tc * 8:(tc + 1) * 8],
                                             PS[4][:, 0:8]),
                             waits=[rmm], inc=("dve", 1))
                ps_rd[4] = lgt
                ev2 = None
                for c8 in range(8):
                    bk = 2 + c8 % 2
                    ttr = P.emit("tensor",
                                 lambda e, o=PS[bk][:, 0:128],
                                 i=x2f[:, c8 * 128:(c8 + 1) * 128], d=identf[:]:
                                 e.transpose(o, i, d),
                                 waits=[ps_rd[bk], tx2f, const_tok],
                                 inc=("pe", 1))
                    ps_wr[bk] = ttr
                    ev2 = P.emit("vector",
                                 vcopy(xn2_fm[:, c8 * T + t0:c8 * T + t0 + 128],
                                       PS[bk][:, 0:128]),
                                 waits=[ttr], inc=("dve", 1))
                    ps_rd[bk] = ev2
                    x2f_rd = ttr
                xn2_tok[tc] = ev2

            slab_tok = [None, None]
            slab_rd = [None, None]
            ub_tok = [None, None]
            va_tok = [None, None]
            rp_tok = [None, None]
            rp_rd = [None, None]
            uvbrow_tok = None
            p1_state = {"prev_q": None, "qpb_rd": None, "kpb_rd": None,
                        "bsf_rd": [None, None]}

            def emit_p1_tile(tt):
                t0 = tt * 512
                xtok = xn2_tok[tt * 4 + 3]
                rsl = 0
                rp_tok[rsl] = P.dma(
                    "sync", ropeb[rsl][:, 0:512], qcos_h[:, t0:t0 + 512],
                    f"srp{rsl}", waits=[rp_rd[rsl]])
                P.dma("sync", ropeb[rsl][:, 512:1024], qsin_h[:, t0:t0 + 512],
                      f"srp{rsl}")
                P.dma("sync", ropeb[rsl][:, 1024:1536], kcos_h[:, t0:t0 + 512],
                      f"srp{rsl}")
                rp_tok[rsl] = P.dma(
                    "sync", ropeb[rsl][:, 1536:2048], ksin_h[:, t0:t0 + 512],
                    f"srp{rsl}")
                for os_ in range(9):
                    sl = os_ % 2
                    ow = 128 if os_ == 8 else 512
                    slab_tok[sl] = P.dma(
                        "scalar",
                        slab[sl][:, 0:8 * ow].rearrange("p (c j) -> p c j", j=ow),
                        uvwTb[:, os_ * 512:os_ * 512 + ow].rearrange(
                            "(c p) j -> p c j", p=128),
                        f"suv{sl}", waits=[slab_rd[sl], uvwT_cast])
                    if os_ < 4:
                        evl = None
                        for ob in range(4):
                            bk = ob % 2
                            mmt = None
                            for dc in range(DC):
                                mmt = P.emit(
                                    "tensor",
                                    mmf(PS[bk][:, 0:512],
                                        slab[sl][:, dc * 512 + ob * 128:
                                                 dc * 512 + (ob + 1) * 128],
                                        xn2_fm[:, dc * T + t0:dc * T + t0 + 512],
                                        dc == 0, dc == 7),
                                    waits=[slab_tok[sl], ps_rd[bk], xtok],
                                    inc=("pe", 1) if dc == 7 else None)
                            ps_wr[bk] = mmt
                            slab_rd[sl] = mmt
                            och = os_ * 4 + ob
                            evl = P.emit(
                                "scalar",
                                lambda e, o=ubuf[sl][:, ob * 512:(ob + 1) * 512],
                                i=PS[bk][:, 0:512], b=cols[:, och:och + 1]:
                                e.activation(o, i, AF.Silu, bias=b),
                                waits=[mmt, ub_tok[sl]], inc=("act", 1))
                            ps_rd[bk] = evl
                        ub_tok[sl] = P.dma(
                            "sync",
                            u_hbm[os_ * 4:(os_ + 1) * 4, :, t0:t0 + 512].rearrange(
                                "c p t -> p c t"),
                            ubuf[sl][:].rearrange("p (c t) -> p c t", t=512),
                            f"suo{sl}", waits=[evl])
                    elif os_ < 8:
                        ev2v = None
                        for tc4 in range(4):
                            bk = tc4 % 2
                            mmt = None
                            for dc in range(DC):
                                mmt = P.emit(
                                    "tensor",
                                    mmf(PS[bk][:, 0:512],
                                        xn2_fm[:, dc * T + t0 + tc4 * 128:
                                               dc * T + t0 + (tc4 + 1) * 128],
                                        slab[sl][:, dc * 512:(dc + 1) * 512],
                                        dc == 0, False),
                                    waits=[slab_tok[sl], ps_rd[bk], xtok],
                                    inc=None)
                            P.emit(
                                "tensor",
                                mmf(PS[bk][:, 0:512],
                                    ones1[:, 0:128],
                                    uvbrow[:, os_ * 512:(os_ + 1) * 512],
                                    False, False),
                                waits=[uvbrow_tok])
                            mmt = P.emit(
                                "tensor",
                                mmf(PS[bk][:, 0:512],
                                    ones1[:, 0:128],
                                    uvbrow_lo[:, os_ * 512:(os_ + 1) * 512],
                                    False, True),
                                inc=("pe", 1))
                            ps_wr[bk] = mmt
                            slab_rd[sl] = mmt
                            ev2v = P.emit(
                                "scalar",
                                lambda e, o=vasm[sl][:, tc4 * 512:(tc4 + 1) * 512],
                                i=PS[bk][:, 0:512]: e.activation(o, i, AF.Silu),
                                waits=[mmt, va_tok[sl]], inc=("act", 1))
                            ps_rd[bk] = ev2v
                        va_tok[sl] = P.dma(
                            "sync",
                            v_hbm[tt * 4:(tt + 1) * 4, :,
                                  (os_ - 4) * 512:(os_ - 3) * 512].rearrange(
                                      "c p t -> p c t"),
                            vasm[sl][:].rearrange("p (c t) -> p c t", t=512),
                            f"svo{sl}", waits=[ev2v])
                    else:
                        st_ = p1_state
                        bk = tt % 2
                        mmt = None
                        for dc in range(DC):
                            mmt = P.emit(
                                "tensor",
                                mmf(PS[bk][:, 0:512],
                                    slab[sl][:, dc * 128:(dc + 1) * 128],
                                    xn2_fm[:, dc * T + t0:dc * T + t0 + 512],
                                    dc == 0, dc == 7),
                                waits=[slab_tok[sl], ps_rd[bk], xtok],
                                inc=("pe", 1) if dc == 7 else None)
                        ps_wr[bk] = mmt
                        slab_rd[sl] = mmt
                        tb = P.emit("scalar",
                                    lambda e, o=bsf[bk][:], i=PS[bk][:, 0:512],
                                    b=cols[:, 32:33]:
                                    e.activation(o, i, AF.Silu, bias=b),
                                    waits=[mmt, st_["bsf_rd"][bk]],
                                    inc=("act", 1))
                        ps_rd[bk] = tb
                        tq = P.emit("vector",
                                    lambda e, o=qpf[:], i=bsf[bk][:],
                                    s1=cols[:, 41:42], s2=cols[:, 43:44]:
                                    e.tensor_scalar(o, i, s1, s2, ALU.mult, ALU.add),
                                    waits=[tb], inc=("dve", 1))
                        tk = P.emit("vector",
                                    lambda e, o=kpf[:], i=bsf[bk][:],
                                    s1=cols[:, 42:43], s2=cols[:, 44:45]:
                                    e.tensor_scalar(o, i, s1, s2, ALU.mult, ALU.add),
                                    inc=("dve", 1))
                        st_["bsf_rd"][bk] = tk
                        tqb = P.emit("scalar", scopy(qpb[:], qpf[:]),
                                     waits=[tq, st_["qpb_rd"]], inc=("act", 1))
                        tkb = P.emit("scalar", scopy(kpb[:], kpf[:]),
                                     waits=[tk, st_["kpb_rd"]], inc=("act", 1))
                        swq = P.emit("tensor", mmf(PS[5][:, 0:512], perm[:], qpb[:],
                                                   True, True),
                                     waits=[tqb, ps_rd[5]], inc=("pe", 1))
                        swk = P.emit("tensor", mmf(PS[6][:, 0:512], perm[:], kpb[:],
                                                   True, True),
                                     waits=[tkb, ps_rd[6]], inc=("pe", 1))
                        st_["qpb_rd"] = swq
                        st_["kpb_rd"] = swk
                        P.emit("vector", lambda e, o=tm1[:], a=qpf[:],
                               b=ropeb[rsl][:, 0:512]: e.tensor_mul(o, a, b),
                               waits=[rp_tok[rsl]])
                        P.emit("vector", lambda e, o=tm2[:], a=PS[5][:, 0:512],
                               b=ropeb[rsl][:, 512:1024]: e.tensor_mul(o, a, b),
                               waits=[swq])
                        tqf = P.emit("vector", lambda e, o=q_fm[:, t0:t0 + 512],
                                     a=tm1[:], b=tm2[:]: e.tensor_add(o, a, b),
                                     inc=("dve", 1))
                        ps_rd[5] = tqf
                        P.emit("vector", lambda e, o=tm1[:], a=kpf[:],
                               b=ropeb[rsl][:, 1024:1536]: e.tensor_mul(o, a, b))
                        P.emit("vector", lambda e, o=tm2[:], a=PS[6][:, 0:512],
                               b=ropeb[rsl][:, 1536:2048]: e.tensor_mul(o, a, b),
                               waits=[swk])
                        tkf = P.emit("vector", lambda e, o=k_fm[:, t0:t0 + 512],
                                     a=tm1[:], b=tm2[:]: e.tensor_add(o, a, b),
                                     inc=("dve", 1))
                        ps_rd[6] = tkf
                        rp_rd[rsl] = tkf
                        st_["prev_q"] = tkf

            if do_p1:
                P.dma("sync", ones1[:], ones1_h, "s_bias")
                lo_t = None
                for ch in range(9):
                    cw = 128 if ch == 8 else 512
                    ft = P.dma("sync", tm1[0:1, 0:cw],
                               uvbrow_h[:, ch * 512:ch * 512 + cw], "s_bias",
                               waits=[lo_t])
                    lo_t = P.emit("vector",
                                  lambda e, o=uvbrow_lo[:, ch * 512:ch * 512 + cw],
                                  a=tm1[0:1, 0:cw],
                                  b=uvbrow[:, ch * 512:ch * 512 + cw]:
                                  e.tensor_sub(o, a, b),
                                  waits=[hi_t, ft], inc=("dve", 1))
                uvbrow_tok = lo_t

            for tt in range(8):
                for tc in range(tt * 4, tt * 4 + 4):
                    emit_p0_chunk(tc)
                if do_p1:
                    emit_p1_tile(tt)
                if tt == 5 and not exp_cast:
                    emit_expert_casts(("dve", P.cnt["dve"], None))

            tkv = xfm32[:, 0:128]
            mb1 = sqb[:, 0:256]
            mb2 = sqb[:, 256:512]
            mb3 = t2[:, 0:256]
            e1b = t2[:, 256:512]
            l2b = x2f[:, 0:256]
            pbb = x2f[:, 256:512]

            def as3(ap):
                return ap.rearrange("p (c e) -> p c e", e=8)

            def bc3(ap):
                return ap.rearrange("p (c e) -> p c e", e=1).broadcast_to(
                    (128, 32, 8))

            lgv = logits[:]
            t1 = P.emit("vector", lambda e, o=tkv[:, 0:32], i=lgv:
                        e.reduce_max(o.rearrange("p (c e) -> p c e", e=1),
                                     as3(i), AX.X), inc=("dve", 1))
            t2_ = P.emit("scalar", lambda e, o=mb1[:], i=tkv[:, 0:32]:
                         e.activation(as3(o), bc3(i), AF.Identity),
                         waits=[t1], inc=("act", 1))
            P.emit("vector", lambda e, o=e1b[:], a=lgv, b=mb1[:]:
                   e.tensor_tensor(o, a, b, ALU.is_equal), waits=[t2_])
            P.emit("vector", lambda e, o=l2b[:], i=e1b[:], j=lgv:
                   e.scalar_tensor_tensor(o, i, -1e30, j, ALU.mult, ALU.add))
            t5 = P.emit("vector", lambda e, o=tkv[:, 32:64], i=l2b[:]:
                        e.reduce_max(o.rearrange("p (c e) -> p c e", e=1),
                                     as3(i), AX.X), inc=("dve", 1))
            t6 = P.emit("scalar", lambda e, o=mb2[:], i=tkv[:, 32:64]:
                        e.activation(as3(o), bc3(i), AF.Identity),
                        waits=[t5], inc=("act", 1))
            P.emit("vector", lambda e, o=l2b[:], a=l2b[:], b=mb2[:]:
                   e.tensor_tensor(o, a, b, ALU.is_equal), waits=[t6])
            t8 = P.emit("vector", lambda e, o=e1b[:], a=e1b[:], b=l2b[:]:
                        e.tensor_add(o, a, b), inc=("dve", 1))
            t9 = P.emit("scalar", lambda e, o=pbb[:], i=lgv:
                        e.activation(o, i, AF.Sigmoid), waits=[t8],
                        inc=("act", 1))
            P.emit("scalar", lambda e, o=tkv[:, 64:96], i=tkv[:, 0:32]:
                   e.activation(o, i, AF.Sigmoid))
            t11 = P.emit("scalar", lambda e, o=tkv[:, 96:128], i=tkv[:, 32:64]:
                         e.activation(o, i, AF.Sigmoid), inc=("act", 1))
            t12 = P.emit("vector", lambda e, o=tkv[:, 0:32], a=tkv[:, 64:96],
                         b=tkv[:, 96:128]: e.tensor_add(o, a, b),
                         waits=[t11], inc=("dve", 1))
            t13 = P.emit("scalar",
                         lambda e, o=tkv[:, 32:64], i=tkv[:, 0:32],
                         b=cols[:, 46:47]:
                         e.activation(o, i, AF.Identity, bias=b),
                         waits=[t12], inc=("act", 1))
            t14 = P.emit("vector", lambda e, o=tkv[:, 64:96], i=tkv[:, 32:64]:
                         e.reciprocal(o, i), waits=[t13], inc=("dve", 1))
            t15 = P.emit("scalar", lambda e, o=mb3[:], i=tkv[:, 64:96]:
                         e.activation(as3(o), bc3(i), AF.Identity),
                         waits=[t14], inc=("act", 1))
            P.emit("vector", lambda e, o=pbb[:], a=pbb[:], b=e1b[:]:
                   e.tensor_mul(o, a, b), waits=[t9])
            P.emit("vector", lambda e, o=comb[:], a=pbb[:], b=mb3[:]:
                   e.tensor_mul(o, a, b), waits=[t15], inc=("dve", 1))
            if "comb_s" in dbg:
                cdb = nc.dram_tensor("comb_s", [128, 256], F32,
                                     kind="ExternalOutput").ap()
                ldb = nc.dram_tensor("log_s", [128, 256], F32,
                                     kind="ExternalOutput").ap()
                P.dma("sync", cdb, comb[:], "sdbg", waits=P.cur_all())
                P.dma("sync", ldb, logits[:], "sdbg")
            if "qk_s" in dbg:
                qk_dbg = nc.dram_tensor("qk_s", [2, 128, T], BF16,
                                        kind="ExternalOutput").ap()
                P.dma("sync", qk_dbg[0], q_fm[:], "sdbg", waits=P.cur_all())
                P.dma("sync", qk_dbg[1], k_fm[:], "sdbg")
            P.join_all()
            c0.close()
            cA.close()

        if "p3" in phases:
            with ExitStack() as c3:
                mbias = sb("mbias_b", [128, MW], ctx=c3)
                kernT = sb("kernT", [128, 32 * 512], BF16, ctx=c3)
                owt3 = sb("owt3", [128, 16 * 1024], BF16, ctx=c3)
                u_sl = sb("u_sl", [128, 16 * 512], BF16, ctx=c3)
                outp = sb("outp", [128, 16 * 512], BF16, ctx=c3)
                vbuf = [sb(f"vbuf{i}", [128, 4 * 512], BF16, ctx=c3) for i in range(2)]
                t1b = [sb(f"t1b{i}", [128, 512], ctx=c3) for i in range(4)]
                rb2 = [sb(f"rb2{i}", [128, 512], ctx=c3) for i in range(4)]
                prj = [sb(f"prj{i}", [128, 512], ctx=c3) for i in range(2)]
                at2a = [sb(f"at2a{i}", [128, 4 * 1024], BF16, ctx=c3) for i in range(2)]

                mb_tok = P.dma("sync", mbias[:], mbias_h, "s_mb")
                owt_tok = P.dma("sync",
                                owt3[:].rearrange("p (c o) -> p c o", o=1024),
                                owTb.rearrange("(c p) o -> p c o", p=128), "sowt",
                                waits=[owT_cast])
                sui_tok = None
                usl_rd = None
                vb_tok = [None, None]
                vb_rd = [None, None]
                at_tok = [None, None]
                outp_rd = None
                kern_tok = [None] * 32
                rb_rd = [None] * 4

                for qt in range(8):
                    q0 = qt * 512
                    sui_tok = P.dma(
                        "sync",
                        u_sl[:].rearrange("p (c t) -> p c t", t=512),
                        u_hbm[:, :, q0:q0 + 512].rearrange("c p t -> p c t"),
                        "sui", waits=[usl_rd])
                    for kt in range(32):
                        bk = 4 + kt % 4
                        sl4 = kt % 4
                        off = 3968 - kt * 128 + q0
                        smm = P.emit("tensor",
                                     mmf(PS[bk][:, 0:512], k_fm[:, kt * 128:(kt + 1) * 128],
                                         q_fm[:, q0:q0 + 512], True, True),
                                     waits=[ps_rd[bk]], inc=("pe", 1))
                        ps_wr[bk] = smm
                        a1 = P.emit("vector",
                                    lambda e, o=t1b[sl4][:], a=PS[bk][:, 0:512],
                                    b=mbias[:, off:off + 512]: e.tensor_add(o, a, b),
                                    waits=[smm, mb_tok], inc=("dve", 1))
                        ps_rd[bk] = a1
                        a2 = P.emit("scalar",
                                    lambda e, o=rb2[sl4][:], i=t1b[sl4][:]:
                                    e.activation(o, i, AF.Relu),
                                    waits=[a1, rb_rd[sl4]], inc=("act", 1))
                        a3 = P.emit("vector",
                                    lambda e, o=kernT[:, kt * 512:(kt + 1) * 512],
                                    a=t1b[sl4][:], b=rb2[sl4][:]:
                                    e.tensor_mul(o, a, b),
                                    waits=[a2], inc=("dve", 1))
                        rb_rd[sl4] = a3
                        kern_tok[kt] = a3
                    for g in range(4):
                        gtok = None
                        for kt in range(32):
                            kb = kt // 4
                            if kt % 4 == 0:
                                vb_tok[kb % 2] = P.dma(
                                    "scalar",
                                    vbuf[kb % 2][:].rearrange(
                                        "p (k e) -> p k e", e=512),
                                    v_hbm[kt:kt + 4, :, g * 512:(g + 1) * 512]
                                    .rearrange("k p e -> p k e"),
                                    f"svi{kb % 2}", waits=[vb_rd[kb % 2]])
                            vsl = vbuf[kb % 2][:, (kt % 4) * 512:(kt % 4) * 512 + 512]
                            for ec4 in range(4):
                                gtok = P.emit(
                                    "tensor",
                                    mmf(PS[ec4][:, 0:512],
                                        vsl[:, ec4 * 128:(ec4 + 1) * 128],
                                        kernT[:, kt * 512:(kt + 1) * 512],
                                        kt == 0, kt == 31, skip=True),
                                    waits=[vb_tok[kb % 2], kern_tok[kt], ps_rd[ec4]],
                                    inc=("pe", 1) if ec4 == 3 else None)
                            if kt % 4 == 3:
                                vb_rd[kb % 2] = gtok
                        for ec4 in range(4):
                            ec = g * 4 + ec4
                            ev = P.emit(
                                "vector",
                                lambda e, o=outp[:, ec * 512:(ec + 1) * 512],
                                a=PS[ec4][:, 0:512], b=u_sl[:, ec * 512:(ec + 1) * 512]:
                                e.tensor_mul(o, a, b),
                                waits=[gtok, sui_tok, outp_rd], inc=("dve", 1))
                            ps_rd[ec4] = ev
                    usl_rd = ps_rd[3]
                    ev_out = ps_rd[3]
                    prj_rd = [None, None]
                    tev = None
                    for hh in range(8):
                        bk = 4 + hh % 2
                        pmm = None
                        for ec in range(16):
                            pmm = P.emit(
                                "tensor",
                                mmf(PS[bk][:, 0:512],
                                    owt3[:, ec * 1024 + hh * 128:ec * 1024 + (hh + 1) * 128],
                                    outp[:, ec * 512:(ec + 1) * 512],
                                    ec == 0, ec == 15),
                                waits=[owt_tok, ev_out, ps_rd[bk]],
                                inc=("pe", 1) if ec == 15 else None)
                        ps_wr[bk] = pmm
                        outp_rd = pmm
                        pe1 = P.emit("scalar",
                                     lambda e, o=prj[hh % 2][:], i=PS[bk][:, 0:512],
                                     b=cols[:, 33 + hh:34 + hh]:
                                     e.activation(o, i, AF.Identity, bias=b),
                                     waits=[pmm, prj_rd[hh % 2]], inc=("act", 1))
                        ps_rd[bk] = pe1
                        for tc4 in range(4):
                            bkT = 6 + tc4 % 2
                            ttr = P.emit("tensor",
                                         lambda e, o=PS[bkT][:, 0:128],
                                         i=prj[hh % 2][:, tc4 * 128:(tc4 + 1) * 128],
                                         d=identf[:]: e.transpose(o, i, d),
                                         waits=[pe1, ps_rd[bkT]], inc=("pe", 1))
                            ps_wr[bkT] = ttr
                            prj_rd[hh % 2] = ttr
                            tev = P.emit(
                                "vector",
                                vcopy(at2a[qt % 2][:, tc4 * 1024 + hh * 128:
                                                   tc4 * 1024 + (hh + 1) * 128],
                                      PS[bkT][:, 0:128]),
                                waits=[ttr, at_tok[qt % 2]], inc=("dve", 1))
                            ps_rd[bkT] = tev
                    at_tok[qt % 2] = P.dma(
                        "sync",
                        at2_hbm[q0:q0 + 512, :].rearrange("(c p) o -> p c o", p=128),
                        at2a[qt % 2][:].rearrange("p (c o) -> p c o", o=1024),
                        f"sat{qt % 2}", waits=[tev])
                P.join_all()

        if not exp_cast:
            emit_expert_casts(None)
        if "ffn" in phases:
            with ExitStack() as cF:
                xn_sl = [sb(f"xnsl{i}", [128, 8 * 512], BF16, ctx=cF) for i in range(2)]
                h_sb = sb("h_sb", [128, 32 * 512], BF16, ctx=cF)
                rbuf = [sb(f"rbuf{i}", [128, 512], ctx=cF) for i in range(2)]
                wu_sl = [sb(f"wusl{i}", [128, 8 * 512], BF16, ctx=cF) for i in range(2)]
                wd_sl = [sb(f"wdsl{i}", [128, 4 * 1024], BF16, ctx=cF) for i in range(2)]
                acc = [sb(f"acc{i}", [128, 4 * 1024], ctx=cF) for i in range(2)]
                r0f = sb("r0f", [128, 4 * 1024], ctx=cF)
                a2f = sb("a2f", [128, 4 * 1024], BF16, ctx=cF)

                xn_tok = [None, None]
                xn_rd = [None, None]
                wu_tok = [None, None]
                wu_rd = [None, None]
                wd_tok = [None, None]
                wd_rd = [None, None]
                st_tok = [None, None]
                rb_rd = [None, None]
                r0_rd = None
                a2_rd = None
                h_rd = None

                for tt in range(8):
                    t0 = tt * 512
                    xn_tok[tt % 2] = P.dma(
                        "sync",
                        xn_sl[tt % 2][:].rearrange("p (c t) -> p c t", t=512),
                        xn_hbm[:, :, t0:t0 + 512].rearrange("c p t -> p c t"),
                        f"sxi{tt % 2}", waits=[xn_rd[tt % 2]])
                    r0_tok = P.dma(
                        "sync",
                        r0f[:].rearrange("p (c o) -> p c o", o=1024),
                        r0_hbm[t0:t0 + 512, :].rearrange("(c p) o -> p c o", p=128),
                        "sri", waits=[r0_rd])
                    a2_tok = P.dma(
                        "sync",
                        a2f[:].rearrange("p (c o) -> p c o", o=1024),
                        at2_hbm[t0:t0 + 512, :].rearrange("(c p) o -> p c o", p=128),
                        "sai", waits=[a2_rd])
                    for ei in range(9):
                        e_idx = None if ei == 0 else ei - 1
                        uh_tok = [None] * 32
                        for js in range(8):
                            sl = js % 2
                            j0 = js * 512
                            if e_idx is None:
                                src = wsub[:, j0:j0 + 512]
                                ctok = sh_cast
                            else:
                                src = wupb[e_idx, :, j0:j0 + 512]
                                ctok = exp_cast[e_idx]
                            wu_tok[sl] = P.dma(
                                "sync",
                                wu_sl[sl][:].rearrange("p (c j) -> p c j", j=512),
                                src.rearrange("(c p) j -> p c j", p=128),
                                f"swu{sl}", waits=[wu_rd[sl], ctok])
                            for jb in range(4):
                                j = js * 4 + jb
                                bk = j % 2
                                umt = None
                                for dc in range(DC):
                                    umt = P.emit(
                                        "tensor",
                                        mmf(PS[bk][:, 0:512],
                                            wu_sl[sl][:, dc * 512 + jb * 128:
                                                       dc * 512 + (jb + 1) * 128],
                                            xn_sl[tt % 2][:, dc * 512:(dc + 1) * 512],
                                            dc == 0, dc == 7),
                                        waits=[wu_tok[sl], xn_tok[tt % 2], ps_rd[bk]],
                                        inc=("pe", 1) if dc == 7 else None)
                                ps_wr[bk] = umt
                                wu_rd[sl] = umt
                                ur = P.emit("scalar",
                                            lambda e, o=rbuf[bk][:], i=PS[bk][:, 0:512]:
                                            e.activation(o, i, AF.Relu),
                                            waits=[umt, rb_rd[bk]], inc=("act", 1))
                                ps_rd[bk] = ur
                                uh = P.emit("vector",
                                            lambda e, o=h_sb[:, j * 512:(j + 1) * 512],
                                            a=rbuf[bk][:], b=rbuf[bk][:]:
                                            e.tensor_mul(o, a, b),
                                            waits=[ur, h_rd], inc=("dve", 1))
                                rb_rd[bk] = uh
                                uh_tok[j] = uh
                        xn_rd[tt % 2] = wu_rd[1] if wu_rd[1] else wu_rd[0]
                        dmt = None
                        for js in range(8):
                            sl = js % 2
                            j0 = js * 512
                            if e_idx is None:
                                srcd = wsdb[j0:j0 + 512, :]
                                ctok = sh_cast
                            else:
                                srcd = wdnb[e_idx, j0:j0 + 512, :]
                                ctok = exp_cast[e_idx]
                            wd_tok[sl] = P.dma(
                                "scalar",
                                wd_sl[sl][:].rearrange("p (c o) -> p c o", o=1024),
                                srcd.rearrange("(c p) o -> p c o", p=128),
                                f"swd{sl}", waits=[wd_rd[sl], ctok])
                            for jc4 in range(4):
                                j = js * 4 + jc4
                                for tc4 in range(4):
                                    for os2 in range(2):
                                        g = tc4 * 2 + os2
                                        dmt = P.emit(
                                            "tensor",
                                            mmf(PS[g][:, 0:512],
                                                h_sb[:, j * 512 + tc4 * 128:
                                                     j * 512 + (tc4 + 1) * 128],
                                                wd_sl[sl][:, jc4 * 1024 + os2 * 512:
                                                          jc4 * 1024 + (os2 + 1) * 512],
                                                js == 0 and jc4 == 0,
                                                js == 7 and jc4 == 3, skip=True),
                                            waits=[wd_tok[sl], uh_tok[j], ps_rd[g]],
                                            inc=("pe", 1) if (jc4 == 3 and g == 7)
                                            else None)
                                wd_rd[sl] = dmt if dmt is not None else wd_rd[sl]
                        h_rd = dmt
                        for tc4 in range(4):
                            for os2 in range(2):
                                g = tc4 * 2 + os2
                                dst = acc[tt % 2][:, tc4 * 1024 + os2 * 512:
                                                  tc4 * 1024 + (os2 + 1) * 512]
                                if e_idx is None:
                                    evt = P.emit("vector", vcopy(dst, PS[g][:, 0:512]),
                                                 waits=[dmt, st_tok[tt % 2]],
                                                 inc=("dve", 1))
                                else:
                                    cidx = (tt * 4 + tc4) * 8 + e_idx
                                    evt = P.emit(
                                        "vector",
                                        lambda e, o=dst, i=PS[g][:, 0:512],
                                        s=comb[:, cidx:cidx + 1]:
                                        e.scalar_tensor_tensor(o, i, s, o,
                                                               ALU.mult, ALU.add),
                                        waits=[dmt], inc=("dve", 1))
                                ps_rd[g] = evt
                    f2 = None
                    for tc4 in range(4):
                        dst = acc[tt % 2][:, tc4 * 1024:(tc4 + 1) * 1024]
                        P.emit("vector", lambda e, o=dst, a=dst,
                               b=r0f[:, tc4 * 1024:(tc4 + 1) * 1024]:
                               e.tensor_add(o, a, b), waits=[r0_tok])
                        f2 = P.emit("vector", lambda e, o=dst, a=dst,
                                    b=a2f[:, tc4 * 1024:(tc4 + 1) * 1024]:
                                    e.tensor_add(o, a, b), waits=[a2_tok],
                                    inc=("dve", 1))
                    r0_rd = f2
                    a2_rd = f2
                    st_tok[tt % 2] = P.dma(
                        "sync",
                        out_h[t0:t0 + 512, :].rearrange("(c p) o -> p c o", p=128),
                        acc[tt % 2][:].rearrange("p (c o) -> p c o", o=1024),
                        f"sst{tt % 2}", waits=[f2])
        P.wait("sync", P.cur_all())

        P.build()
    return nc, P


def _host_tables(ins):
    import jax
    import jax.numpy as jnp

    w = np.asarray(ins["L0_attn_w"], np.float32)
    raw = np.arange(128)[:, None] - np.arange(MW)[None, :] + 4479
    idx = np.clip(np.where(raw < 0, raw + 1023, raw), 0, 1022)
    M = w[idx].astype(np.float32)

    with jax.default_device(jax.devices("cpu")[0]):
        pos_j = jnp.arange(T, dtype=jnp.float32)
        inv_j = 10000.0 ** (jnp.arange(64, dtype=jnp.float32) / 64)
        sinus = pos_j[:, None] * inv_j[None, :]
        sin_t = np.asarray(jnp.sin(sinus))
        cos_t = np.asarray(jnp.cos(sinus))
    cos_dup = np.ascontiguousarray(np.concatenate([cos_t, cos_t], 1).T)
    sinsgn = np.ascontiguousarray(np.concatenate([-sin_t, sin_t], 1).T)
    SQ = np.float32(1.0 / np.sqrt(128.0))

    cols = np.zeros((128, 64), np.float32)
    cols[:, 0:33] = np.asarray(ins["L0_attn_uv_b"], np.float32).reshape(33, 128).T
    cols[:, 33:41] = np.asarray(ins["L0_attn_o_b"], np.float32).reshape(8, 128).T
    g = np.asarray(ins["L0_attn_gamma"], np.float32)
    b = np.asarray(ins["L0_attn_beta"], np.float32)
    cols[:, 41] = g[0]
    cols[:, 42] = g[1]
    cols[:, 43] = b[0]
    cols[:, 44] = b[1]
    cols[:, 45] = EPS
    cols[:, 46] = 1e-6
    cols[:, 47] = -0.5
    cols[:, 48] = 1.0 / D

    permm = np.zeros((128, 128), np.float32)
    permm[(np.arange(128) + 64) % 128, np.arange(128)] = 1.0

    return {
        "uvwT": np.ascontiguousarray(np.asarray(ins["L0_attn_uv_w"], np.float32).T),
        "owT": np.ascontiguousarray(np.asarray(ins["L0_attn_o_w"], np.float32).T),
        "wr": np.ascontiguousarray(np.asarray(ins["L0_ffn_W_router"], np.float32)),
        "wup": np.ascontiguousarray(np.asarray(ins["L0_ffn_W_up"], np.float32)),
        "wdn": np.ascontiguousarray(np.asarray(ins["L0_ffn_W_down"], np.float32)),
        "wsu": np.ascontiguousarray(np.asarray(ins["L0_ffn_W_sh_up"], np.float32)),
        "wsd": np.ascontiguousarray(np.asarray(ins["L0_ffn_W_sh_down"], np.float32)),
        "cols": cols,
        "lnw": np.ascontiguousarray(
            np.broadcast_to(np.asarray(ins["L0_attn_ln_w"], np.float32), (128, D))),
        "lnb": np.ascontiguousarray(
            np.broadcast_to(np.asarray(ins["L0_attn_ln_b"], np.float32), (128, D))),
        "mbias": M,
        "qcos": np.ascontiguousarray(cos_dup * SQ),
        "qsin": np.ascontiguousarray(sinsgn * SQ),
        "kcos": cos_dup,
        "ksin": sinsgn,
        "uvbrow_f": np.asarray(ins["L0_attn_uv_b"], np.float32).reshape(1, 4224).copy(),
        "ones1": np.ones((1, 128), dtype=ml_dtypes.bfloat16),
        "perm": permm.astype(ml_dtypes.bfloat16),
        "identf": np.eye(128, dtype=np.float32),
    }


def kernel(**ins):
    tabs = _host_tables(ins)
    x = np.asarray(ins["L0_attn_x"], np.float32)
    nc, _ = build_nc()
    in_maps = []
    for c in range(8):
        m = dict(tabs)
        m["x"] = np.ascontiguousarray(x[c])
        in_maps.append(m)
    res = run_bass_kernel_spmd(nc, in_maps, core_ids=list(range(8)))
    return np.stack([res.results[c]["out"] for c in range(8)], axis=0)
